# version 04
# baseline (speedup 1.0000x reference)
import math
from contextlib import ExitStack

import numpy as np
import ml_dtypes
import concourse.bass as bass
import concourse.mybir as mybir
from concourse.bass_utils import run_bass_kernel_spmd
from concourse.expressions import RuntimeValue

F32 = mybir.dt.float32
BF16 = mybir.dt.bfloat16
FP8 = mybir.dt.float8e4
ALU = mybir.AluOpType
ACT = mybir.ActivationFunctionType
AX = mybir.AxisListType.X

D = 2048
H = 16
DH = 128
B, T = 2, 4096
NT = 1024
NTT = NT // 128
KC = D // 128
PH = DH * NT
KV_HALF = H * PH
EXP_BIAS = -12.0
SCALE_S = 1.0 / math.sqrt(DH)
RGROUPS = [[0, 1, 2, 3], [4, 5, 6, 7]]


class Ctr:
    def __init__(self, sem, step=1):
        self.sem, self.v, self.step = sem, 0, step

    def inc(self, inst, n=None):
        n = self.step if n is None else n
        inst.then_inc(self.sem, n)
        self.v += n
        return self.v


def build_nc():
    nc = bass.Bass("TRN2", target_bir_lowering=False)

    x_h = nc.dram_tensor("x_own", [NT, D], F32, kind="ExternalInput").ap()
    wqkv_h = nc.dram_tensor("wqkvT", [D, 3 * D], F32, kind="ExternalInput").ap()
    wo_h = nc.dram_tensor("woT", [D, D], F32, kind="ExternalInput").ap()
    inp2_h = nc.dram_tensor("inp2", [D, D], F32, kind="ExternalInput").ap()
    cos_h = nc.dram_tensor("cosT", [128, NT], F32, kind="ExternalInput").ap()
    sin_h = nc.dram_tensor("sinT", [128, NT], F32, kind="ExternalInput").ap()
    mask_h = nc.dram_tensor("maskT", [128, 4 * 512], BF16, kind="ExternalInput").ap()
    ones_h = nc.dram_tensor("onesb", [128, 128], BF16, kind="ExternalInput").ap()
    onesf_h = nc.dram_tensor("onesf", [1, 128], F32, kind="ExternalInput").ap()
    scl_h = nc.dram_tensor("scale_b", [128, 1], F32, kind="ExternalInput").ap()
    scl1_h = nc.dram_tensor("scale1_b", [128, 1], F32, kind="ExternalInput").ap()
    y_h = nc.dram_tensor("y_own", [NT, D], F32, kind="ExternalOutput").ap()

    kv_own = nc.dram_tensor("kv_own", [2 * KV_HALF], BF16, kind="Internal").ap()
    kv_all = nc.dram_tensor("kv_all", [4, 2 * KV_HALF], BF16, kind="Internal").ap()

    ctx = ExitStack()
    E = ctx.enter_context

    cos_t = E(nc.sbuf_tensor("cos_t", [128, NT], F32))
    sin_t = E(nc.sbuf_tensor("sin_t", [128, NT], F32))
    masks = E(nc.sbuf_tensor("masks_t", [128, 4 * 512], BF16))
    scl = E(nc.sbuf_tensor("scl_t", [128, 1], F32))
    scl1 = E(nc.sbuf_tensor("scl1_t", [128, 1], F32))
    onesb = E(nc.sbuf_tensor("onesb_t", [128, 128], BF16))
    onesf = E(nc.sbuf_tensor("onesf_t", [1, 128], F32))
    nbias = E(nc.sbuf_tensor("nbias_t", [128, 1], F32))
    wbuf = [E(nc.sbuf_tensor(f"wbuf{i}", [128, KC * 512], BF16)) for i in range(2)]
    q_fm = E(nc.sbuf_tensor("q_fm", [128, H * NT], BF16))
    av0 = E(nc.sbuf_tensor("av0", [128, NT], BF16))
    expT = [E(nc.sbuf_tensor(f"expT{i}", [128, 512], BF16)) for i in range(3)]
    kst = [E(nc.sbuf_tensor(f"kst{i}", [128, NT], BF16)) for i in range(2)]
    vst = [E(nc.sbuf_tensor(f"vst{i}", [128, 512], BF16)) for i in range(2)]
    bc_sb = E(nc.sbuf_tensor("bc_sb", [128, 512], F32))
    sums_sb = E(nc.sbuf_tensor("sums_sb", [1, 512], F32))
    recip_sb = E(nc.sbuf_tensor("recip_sb", [1, 512], F32))
    stt = {n: E(nc.sbuf_tensor(f"st_{n}", [128, 1], F32))
           for n in ["sum", "sq", "mu", "nmu", "ve", "t0", "t1", "rs", "rss"]}
    rr1 = E(nc.sbuf_tensor("rr1", [128, 1], F32))

    AR = 110 * 1024
    arena = nc.alloc_sbuf_tensor("arena", [128, AR], mybir.dt.uint8)
    abase = nc.lookup_mloc(arena).addr

    def at(name, off_kb, shape, dtype):
        return nc.alloc_sbuf_tensor_at(name, shape, dtype,
                                       offset=abase + off_kb * 1024)

    h1_fm = at("h1_fm", 0, [128, KC * NT], BF16)
    xt = [at(f"xt{i}", 32 + 8 * i, [128, D], F32) for i in range(2)]
    sq_sc = at("sq_sc", 48, [128, D], F32)
    h1tm = [at(f"h1tm{i}", 56 + 4 * i, [128, D], BF16) for i in range(2)]
    rp_c = at("rp_c", 32, [128, NT], F32)
    rp_s = at("rp_s", 36, [128, NT], F32)
    kbuf = [at(f"kbuf{i}", 8 * i, [128, 4096], BF16) for i in range(2)]
    vbuf = [at(f"vbuf{i}", 16 + 8 * i, [128, 4096], BF16) for i in range(2)]
    q2_f8 = at("q2_f8", 0, [128, KC * D], FP8)
    x2_sb = at("x2_sb", 32, [128, NTT * D], F32)
    i2st = at("i2st", 96, [128, D], F32)
    q1tm8 = at("q1tm8", 96, [128, D], FP8)
    q1tmb = at("q1tmb", 98, [128, D], BF16)
    ost = at("ost", 102, [128, D], F32)

    S = {n: nc.alloc_semaphore(n) for n in [
        "x", "tr", "wg", "pe2", "ve2", "kvw", "cc", "kvl", "as_", "ax", "msk",
        "avs", "smv", "ve3", "bcc", "bc", "nrm", "ph", "pe4", "ve4", "i2l",
        "q2", "pe5", "ve5", "tr5", "ln", "lnw", "st", "xr", "veh",
        "sx0", "sx1", "wg0", "wg1", "kw0", "kw1", "vw0", "vw1", "kl0", "kl1",
        "t50", "t51", "tr0", "tr1"]}

    c_x = Ctr(S["x"], 16)
    c_sx = [Ctr(S["sx0"], 16), Ctr(S["sx1"], 16)]
    c_tr = [Ctr(S["tr0"], 16), Ctr(S["tr1"], 16)]
    c_wg = [Ctr(S["wg0"], 16), Ctr(S["wg1"], 16)]
    c_kw = [Ctr(S["kw0"], 16), Ctr(S["kw1"], 16)]
    c_vw = [Ctr(S["vw0"], 16), Ctr(S["vw1"], 16)]
    c_t5 = [Ctr(S["t50"], 16), Ctr(S["t51"], 16)]
    c_pe2 = Ctr(S["pe2"])
    c_ve2 = Ctr(S["ve2"])
    c_kvw = Ctr(S["kvw"], 16)
    c_cc = Ctr(S["cc"])
    c_ln = Ctr(S["ln"])
    c_lnw = Ctr(S["lnw"])
    c_veh = Ctr(S["veh"])

    def q_sl(h, a, b):
        return q_fm[:, h * NT + a: h * NT + b]

    def av_sl(h, a, b):
        if h == 0:
            return av0[:, a:b]
        return q_fm[:, (h - 1) * NT + a: (h - 1) * NT + b]

    def q1_sl(c, a, b):
        w = wbuf[c // 8]
        cc_ = c % 8
        return w[:, cc_ * NT + a: cc_ * NT + b]

    TE, VE, SE, SY, GP = nc.tensor, nc.vector, nc.scalar, nc.sync, nc.gpsimd

    def HOP(inst):
        c_veh.inc(inst)
        VE.wait_ge(S["veh"], c_veh.v)
        return inst

    for dst, src in [(cos_t, cos_h), (sin_t, sin_h), (masks, mask_h),
                     (onesb, ones_h), (onesf, onesf_h),
                     (scl, scl_h), (scl1, scl1_h)]:
        c_x.inc(SY.dma_start(out=dst[:], in_=src[:]))
    CONST_LOADS = c_x.v

    VE.wait_ge(S["x"], CONST_LOADS)
    VE.memset(nbias[:], EXP_BIAS)
    HOP(VE.tensor_mul(stt["t0"][:], scl[:], scl1[:]))
    c_ve2.inc(VE.reciprocal(rr1[:], stt["t0"][:]))

    for tt in range(NTT):
        xcur = xt[tt % 2]
        if tt >= 2:
            SY.wait_ge(S["lnw"], tt - 1)
        c_sx[tt % 2].inc(SY.dma_start(out=xcur[:], in_=x_h[tt * 128:(tt + 1) * 128, :]))

        SE.wait_ge(S["x"], CONST_LOADS)
        SE.wait_ge(c_sx[tt % 2].sem, c_sx[tt % 2].v)
        if tt > 0:
            SE.wait_ge(S["lnw"], tt)
        c_ln.inc(SE.activation(sq_sc[:], xcur[:], ACT.Square, accum_out=stt["sq"][:]))

        VE.wait_ge(c_sx[tt % 2].sem, c_sx[tt % 2].v)
        VE.wait_ge(S["ln"], c_ln.v)
        HOP(VE.reduce_sum(stt["sum"][:], xcur[:], axis=AX))
        HOP(VE.tensor_scalar(stt["mu"][:], stt["sum"][:], 1.0 / D, None, ALU.mult))
        HOP(VE.tensor_scalar(stt["nmu"][:], stt["mu"][:], -1.0, None, ALU.mult))
        HOP(VE.tensor_mul(stt["t0"][:], stt["mu"][:], stt["mu"][:]))
        HOP(VE.tensor_scalar(stt["t1"][:], stt["sq"][:], 1.0 / D, None, ALU.mult))
        HOP(VE.tensor_sub(stt["ve"][:], stt["t1"][:], stt["t0"][:]))
        HOP(VE.tensor_scalar(stt["ve"][:], stt["ve"][:], 1e-5, None, ALU.add))

        c_ve2.inc(VE.engine_nop())
        SE.wait_ge(S["ve2"], c_ve2.v)
        c_ln.inc(SE.activation(stt["t0"][:], stt["ve"][:], ACT.Sqrt))
        VE.wait_ge(S["ln"], c_ln.v)
        HOP(VE.reciprocal(stt["rs"][:], stt["t0"][:]))
        HOP(VE.tensor_mul(stt["t0"][:], stt["rs"][:], stt["rs"][:]))
        HOP(VE.tensor_mul(stt["t1"][:], stt["t0"][:], stt["ve"][:]))
        HOP(VE.tensor_scalar(stt["t1"][:], stt["t1"][:], -0.5, 1.5, ALU.mult, ALU.add))
        HOP(VE.tensor_mul(stt["rs"][:], stt["rs"][:], stt["t1"][:]))

        c_ve2.inc(VE.engine_nop())
        VE.wait_ge(c_tr[tt % 2].sem, c_tr[tt % 2].v)
        c_lnw.inc(VE.tensor_scalar(h1tm[tt % 2][:], xcur[:], stt["nmu"][:],
                                   stt["rs"][:], ALU.add, ALU.mult))
        SE.wait_ge(S["lnw"], c_lnw.v)
        for c in range(KC):
            c_tr[tt % 2].inc(SE.dma_start_transpose(
                h1_fm[:, c * NT + tt * 128: c * NT + (tt + 1) * 128],
                h1tm[tt % 2][:, c * 128:(c + 1) * 128]))
    H1_TR = (c_tr[0].v, c_tr[1].v)

    OGS = [4, 5, 6, 7, 8, 9, 10, 11, 0, 1, 2, 3]

    def emit_wload(src_ap, og, buf):
        for kg in range(4):
            src = src_ap[kg * 512:(kg + 1) * 512, og * 512:(og + 1) * 512] \
                .rearrange("(k p) c -> p k c", p=128)
            dst = wbuf[buf][:, kg * 4 * 512:(kg + 1) * 4 * 512] \
                .rearrange("p (k c) -> p k c", k=4)
            c_wg[buf].inc(GP.dma_start(out=dst, in_=src))

    emit_wload(wqkv_h, OGS[0], 0)
    emit_wload(wqkv_h, OGS[1], 1)

    with nc.psum_tensor("ps_qk0", [128, NT], F32) as pqk0, \
         nc.psum_tensor("ps_qk1", [128, NT], F32) as pqk1, \
         nc.psum_tensor("ps_v0", [128, 512], F32) as pv0, \
         nc.psum_tensor("ps_v1", [128, 512], F32) as pv1:
        pqk = [pqk0, pqk1]
        pv = [pv0, pv1]
        n_kvw = 0
        n_kvw_at_ag = [48]
        last_qk_evac = [0, 0]
        last_rope_add = 0
        last_v_evac = [0, 0]
        last_kst_dma = [0, 0]
        last_vst_dma = [0, 0]
        qk_i = 0
        v_i = 0

        TE.wait_ge(c_tr[0].sem, H1_TR[0])
        TE.wait_ge(c_tr[1].sem, H1_TR[1])

        for ogi, og in enumerate(OGS):
            buf = ogi % 2
            TE.wait_ge(c_wg[buf].sem, 16 * 4 * (ogi // 2 + 1))
            if 8 <= og <= 11:
                vhb = (og - 8) * 4
                vsec = kv_own[KV_HALF + vhb * PH: KV_HALF + (vhb + 4) * PH] \
                    .rearrange("(h t d) -> h t d", h=4, d=128)
                for tt in range(NTT):
                    par = v_i % 2
                    if last_v_evac[par]:
                        TE.wait_ge(S["ve2"], last_v_evac[par])
                    for k in range(KC):
                        mm = TE.matmul(
                            pv[par][:],
                            h1_fm[:, k * NT + tt * 128: k * NT + (tt + 1) * 128],
                            wbuf[buf][:, k * 512:(k + 1) * 512],
                            start=(k == 0), stop=(k == KC - 1))
                    c_pe2.inc(mm)
                    VE.wait_ge(S["pe2"], c_pe2.v)
                    sp = tt % 2
                    if last_vst_dma[sp]:
                        VE.wait_ge(c_vw[sp].sem, last_vst_dma[sp])
                    last_v_evac[par] = c_ve2.inc(VE.tensor_copy(vst[sp][:], pv[par][:]))
                    SY.wait_ge(S["ve2"], c_ve2.v)
                    c_vw[sp].inc(SY.dma_start(
                        out=vsec[:, tt * 128:(tt + 1) * 128, :].transpose([1, 0, 2]),
                        in_=vst[sp][:].rearrange("p (h d) -> p h d", h=4)))
                    last_vst_dma[sp] = c_vw[sp].v
                    n_kvw += 1
                    v_i += 1
            else:
                for jj in range(4):
                    par = qk_i % 2
                    if last_qk_evac[par]:
                        TE.wait_ge(S["ve2"], last_qk_evac[par])
                    for tb in range(2):
                        for k in range(KC):
                            mm = TE.matmul(
                                pqk[par][:, tb * 512:(tb + 1) * 512],
                                wbuf[buf][:, k * 512 + jj * 128: k * 512 + jj * 128 + 128],
                                h1_fm[:, k * NT + tb * 512: k * NT + (tb + 1) * 512],
                                start=(k == 0), stop=(k == KC - 1))
                        c_pe2.inc(mm)
                    VE.wait_ge(S["pe2"], c_pe2.v)
                    src = pqk[par]
                    if last_rope_add:
                        VE.wait_ge(S["ve2"], last_rope_add)
                    VE.tensor_mul(rp_c[:], src[:], cos_t[:])
                    VE.tensor_mul(rp_s[0:64, :], src[64:128, :], sin_t[0:64, :])
                    HOP(VE.tensor_mul(rp_s[64:128, :], src[0:64, :], sin_t[64:128, :]))
                    if og >= 4:
                        h = (og - 4) * 4 + jj
                        sp = h % 2
                        if last_kst_dma[sp]:
                            VE.wait_ge(c_kw[sp].sem, last_kst_dma[sp])
                        last_qk_evac[par] = c_ve2.inc(
                            VE.tensor_add(kst[sp][:], rp_c[:], rp_s[:]))
                        last_rope_add = c_ve2.v
                        SY.wait_ge(S["ve2"], c_ve2.v)
                        c_kw[sp].inc(SY.dma_start(
                            out=kv_own[h * PH: (h + 1) * PH]
                            .rearrange("(d t) -> d t", d=128),
                            in_=kst[sp][:]))
                        last_kst_dma[sp] = c_kw[sp].v
                        n_kvw += 1
                    else:
                        h = og * 4 + jj
                        last_qk_evac[par] = c_ve2.inc(
                            VE.tensor_add(q_sl(h, 0, NT), rp_c[:], rp_s[:]))
                        last_rope_add = c_ve2.v
                    qk_i += 1
            if ogi + 2 < len(OGS):
                GP.wait_ge(S["pe2"], c_pe2.v)
                emit_wload(wqkv_h, OGS[ogi + 2], buf)
            if ogi == 9:
                GP.wait_ge(c_kw[0].sem, c_kw[0].v)
                GP.wait_ge(c_kw[1].sem, c_kw[1].v)
                GP.wait_ge(c_vw[0].sem, c_vw[0].v)
                GP.wait_ge(c_vw[1].sem, c_vw[1].v)
                c_cc.inc(GP.collective_compute(
                    "AllGather", ALU.bypass, replica_groups=RGROUPS,
                    ins=[kv_own[:]], outs=[kv_all[:]]))

    QKV_VE = c_ve2.v
    QKV_PE = c_pe2.v

    engines = [TE, VE, SE, SY, GP]
    pid = nc.partition_id()
    idx = []
    for eng in engines:
        reg = eng.alloc_register(f"jreg_{eng.engine.value}")
        eng.reg_mov(reg, pid)
        eng.reg_alu(reg, reg, 4, ALU.mod)
        idx.append(RuntimeValue(reg, min_val=0, max_val=3))

    with nc.psum_tensor("ps_s0", [128, 512], F32) as ps_s0, \
         nc.psum_tensor("ps_s1", [128, 512], F32) as ps_s1, \
         nc.psum_tensor("ps_a0", [128, 512], F32) as ps_a0, \
         nc.psum_tensor("ps_a1", [128, 512], F32) as ps_a1, \
         nc.psum_tensor("ps_m0", [1, 512], F32) as ps_m0, \
         nc.psum_tensor("ps_m1", [1, 512], F32) as ps_m1, \
         nc.psum_tensor("ps_bc", [128, 512], F32) as ps_bc:
        ps_s = [ps_s0, ps_s1]
        ps_a = [ps_a0, ps_a1]
        ps_m = [ps_m0, ps_m1]

        VEH_BASE = c_veh.v
        veh_body_counts = []
        for j in nc.Switch(engines=engines, index=idx, n=4):
            a_veh = Ctr(S["veh"])
            a_veh.v = VEH_BASE

            def HOPB(inst):
                a_veh.inc(inst)
                VE.wait_ge(S["veh"], a_veh.v)
                return inst

            a_kvl = [Ctr(S["kl0"], 16), Ctr(S["kl1"], 16)]
            a_as = Ctr(S["as_"])
            a_ax = Ctr(S["ax"])
            a_msk = Ctr(S["msk"])
            a_avs = Ctr(S["avs"])
            a_smv = Ctr(S["smv"])
            a_ve3 = Ctr(S["ve3"])
            a_bcc = Ctr(S["bcc"])
            a_bc = Ctr(S["bc"])
            a_nrm = Ctr(S["nrm"])
            a_xr = Ctr(S["xr"], 16)

            NBLK = 8 - j
            blocks = [j, 7 - j]
            lph = 2 * NBLK
            av_of_pair = {}
            pend_bcast = []

            def emit_kv_loads(h):
                hb = h % 2
                for g in range(NBLK):
                    r = g if g < 4 else 7 - g
                    half = 0 if g < 4 else 1
                    ksec = kv_all[r, h * PH: (h + 1) * PH] \
                        .rearrange("(d t) -> d t", d=128)
                    a_kvl[hb].inc(SY.dma_start(
                        out=kbuf[hb][:, g * 512:(g + 1) * 512],
                        in_=ksec[:, half * 512:(half + 1) * 512]))
                    vsec = kv_all[r, KV_HALF + h * PH: KV_HALF + (h + 1) * PH] \
                        .rearrange("(t d) -> t d", d=128)
                    a_kvl[hb].inc(SY.dma_start(
                        out=vbuf[hb][:, g * 512:(g + 1) * 512]
                        .rearrange("p (q d) -> p q d", q=4),
                        in_=vsec[half * 512:(half + 1) * 512, :]
                        .rearrange("(q p) d -> p q d", p=128)))

            def emit_bcast(p):
                h_, qb_, par_ = av_of_pair[p]
                TE.wait_ge(S["ve3"], p + 1)
                if p > 0:
                    TE.wait_ge(S["bcc"], p)
                a_bc.inc(TE.matmul(ps_bc[:], onesf[:, :], recip_sb[:, :],
                                   start=True, stop=True, skip_group_check=True))
                VE.wait_ge(S["bc"], a_bc.v)
                if a_nrm.v:
                    VE.wait_ge(S["nrm"], a_nrm.v)
                HOPB(VE.tensor_copy(bc_sb[:], ps_bc[:]))
                a_bcc.inc(VE.engine_nop())
                qoff = qb_ * 512
                a_nrm.inc(VE.tensor_mul(av_sl(h_, qoff, qoff + 512),
                                        ps_a[par_][:], bc_sb[:]))

            SY.wait_ge(S["cc"], 1)
            SY.wait_ge(S["pe2"], QKV_PE)
            SY.wait_ge(S["ve2"], QKV_VE)
            SE.wait_ge(S["ve2"], QKV_VE)
            emit_kv_loads(0)
            emit_kv_loads(1)

            gi = 0
            pairidx = 0
            xr_tt = 0
            for h in range(H):
                TE.wait_ge(a_kvl[h % 2].sem, 16 * lph * (h // 2 + 1))
                for qbi, jb in enumerate(blocks):
                    P = 4 * (jb + 1)
                    par = pairidx % 2
                    qoff = qbi * 512
                    while pend_bcast and pend_bcast[0] <= pairidx - 2:
                        emit_bcast(pend_bcast.pop(0))
                    first_gi = gi
                    for kt in range(P):
                        if gi >= 2:
                            TE.wait_ge(S["ax"], gi - 1)
                        a_as.inc(TE.matmul(ps_s[gi % 2][:],
                                           kbuf[h % 2][:, kt * 128:(kt + 1) * 128],
                                           q_sl(h, qoff, qoff + 512),
                                           start=True, stop=True,
                                           skip_group_check=True))
                        SE.wait_ge(S["as_"], a_as.v)
                        if gi >= 3:
                            SE.wait_ge(S["avs"], 2 * (gi - 2))
                        a_ax.inc(SE.activation(expT[gi % 3][:], ps_s[gi % 2][:],
                                               ACT.Exp, bias=nbias[:], scale=SCALE_S))
                        diag = kt >= P - 4
                        if diag:
                            dd = kt - (P - 4)
                            VE.wait_ge(S["ax"], a_ax.v)
                            a_msk.inc(VE.tensor_mul(
                                expT[gi % 3][:], expT[gi % 3][:],
                                masks[:, dd * 512:(dd + 1) * 512]))
                            TE.wait_ge(S["msk"], a_msk.v)
                        else:
                            TE.wait_ge(S["ax"], a_ax.v)
                        if kt == 0 and pairidx >= 2:
                            TE.wait_ge(S["nrm"], pairidx - 1)
                        a_avs.inc(TE.matmul(ps_a[par][:],
                                            vbuf[h % 2][:, kt * 128:(kt + 1) * 128],
                                            expT[gi % 3][:],
                                            start=(kt == 0), stop=(kt == P - 1),
                                            skip_group_check=True))
                        if kt == 0 and pairidx >= 2:
                            TE.wait_ge(S["smv"], pairidx - 1)
                        a_avs.inc(TE.matmul(ps_m[par][:],
                                            onesb[:, 0:1],
                                            expT[gi % 3][:],
                                            start=(kt == 0), stop=(kt == P - 1),
                                            skip_group_check=True))
                        gi += 1
                        if pend_bcast and (gi - first_gi) == 8:
                            emit_bcast(pend_bcast.pop(0))
                    while pend_bcast and pend_bcast[0] <= pairidx - 1:
                        emit_bcast(pend_bcast.pop(0))
                    SE.wait_ge(S["avs"], a_avs.v)
                    if a_ve3.v:
                        SE.wait_ge(S["ve3"], a_ve3.v)
                        SE.wait_ge(S["bc"], a_bc.v)
                    a_smv.inc(SE.activation(sums_sb[:], ps_m[par][:], ACT.Ln))
                    SE.wait_ge(S["smv"], a_smv.v)
                    a_ve3.inc(SE.activation(recip_sb[:], sums_sb[:], ACT.Exp,
                                            scale=-1.0))
                    av_of_pair[pairidx] = (h, qbi, par)
                    pend_bcast.append(pairidx)
                    pairidx += 1
                if h + 2 < H:
                    SY.wait_ge(S["avs"], a_avs.v)
                    emit_kv_loads(h + 2)
                if xr_tt < NTT:
                    a_xr.inc(SY.dma_start(
                        out=x2_sb[:, xr_tt * D:(xr_tt + 1) * D],
                        in_=x_h[xr_tt * 128:(xr_tt + 1) * 128, :]))
                    xr_tt += 1
            while pend_bcast:
                emit_bcast(pend_bcast.pop(0))

            TE.wait_ge(S["bc"], a_bc.v)
            TE.sem_inc(S["ph"], 1)
            VE.wait_ge(S["nrm"], a_nrm.v)
            VE.sem_inc(S["ph"], 1)
            SE.wait_ge(S["ax"], a_ax.v)
            SE.sem_inc(S["ph"], 1)
            SY.wait_ge(a_kvl[0].sem, a_kvl[0].v)
            SY.wait_ge(a_kvl[1].sem, a_kvl[1].v)
            SY.wait_ge(S["xr"], a_xr.v)
            SY.sem_inc(S["ph"], 1)
            GP.sem_inc(S["ph"], 1)
            veh_body_counts.append(a_veh.v - VEH_BASE)

    assert len(set(veh_body_counts)) == 1, veh_body_counts
    c_veh.v = VEH_BASE + veh_body_counts[0]

    c_pe4 = Ctr(S["pe4"])
    c_ve4 = Ctr(S["ve4"])
    c_i2l = Ctr(S["i2l"], 16)
    c_q2 = Ctr(S["q2"])

    for eng in engines:
        eng.wait_ge(S["ph"], 5)

    def emit_wo(db, buf):
        for kg in range(4):
            src = wo_h[kg * 512:(kg + 1) * 512, db * 512:(db + 1) * 512] \
                .rearrange("(k p) c -> p k c", p=128)
            dst = wbuf[buf][:, kg * 4 * 512:(kg + 1) * 4 * 512] \
                .rearrange("p (k c) -> p k c", k=4)
            c_wg[buf].inc(GP.dma_start(out=dst, in_=src))

    WG0 = [c_wg[0].v, c_wg[1].v]
    emit_wo(0, 0)
    emit_wo(1, 1)

    def emit_i2(k):
        SE.wait_ge(S["q2"], k)
        c_i2l.inc(SE.dma_start(out=i2st[:], in_=inp2_h[k * 128:(k + 1) * 128, :]))
        VE.wait_ge(S["i2l"], c_i2l.v)
        c_q2.inc(VE.tensor_scalar(q2_f8[:, k * D:(k + 1) * D], i2st[:],
                                  scl1[:], None, ALU.mult))

    next_k = [0]

    def maybe_i2():
        if next_k[0] < KC:
            emit_i2(next_k[0])
            next_k[0] += 1

    with nc.psum_tensor("ps_o0", [128, 512], F32) as ps_o0, \
         nc.psum_tensor("ps_o1", [128, 512], F32) as ps_o1:
        ps_o = [ps_o0, ps_o1]
        ot = 0
        for db in range(4):
            TE.wait_ge(c_wg[db % 2].sem, WG0[db % 2] + 16 * 4 * (db // 2 + 1))
            for tt in range(NTT):
                par = ot % 2
                if ot >= 2:
                    TE.wait_ge(S["ve4"], ot - 1)
                for c in range(KC):
                    mm = TE.matmul(ps_o[par][:],
                                   av_sl(c, tt * 128, (tt + 1) * 128),
                                   wbuf[db % 2][:, c * 512:(c + 1) * 512],
                                   start=(c == 0), stop=(c == KC - 1))
                c_pe4.inc(mm)
                VE.wait_ge(S["pe4"], c_pe4.v)
                VE.wait_ge(S["xr"], 16 * NTT)
                xsl = x2_sb[:, tt * D + db * 512: tt * D + (db + 1) * 512]
                c_ve4.inc(VE.tensor_add(xsl, ps_o[par][:], xsl))
                ot += 1
                if ot % 2 == 0:
                    maybe_i2()
            if db + 2 < 4:
                GP.wait_ge(S["pe4"], c_pe4.v)
                emit_wo(db + 2, db % 2)
    while next_k[0] < KC:
        maybe_i2()

    c_pe5 = Ctr(S["pe5"])
    c_ve5 = Ctr(S["ve5"])
    c_tr5 = Ctr(S["tr5"], 16)
    c_st = Ctr(S["st"], 16)

    with nc.psum_tensor("ps_f0", [128, 512], F32) as ps_f0, \
         nc.psum_tensor("ps_f1", [128, 512], F32) as ps_f1, \
         nc.psum_tensor("ps_sq", [128, D], F32) as ps_sq:
        ps_f = [ps_f0, ps_f1]
        ft = 0
        for tt in range(NTT):
            x2t = x2_sb[:, tt * D:(tt + 1) * D]
            SE.wait_ge(S["ve4"], c_ve4.v)
            if tt > 0:
                SE.wait_ge(S["lnw"], c_lnw.v)
            c_ln.inc(SE.activation(ps_sq[:], x2t, ACT.Square, accum_out=stt["sq"][:]))
            VE.wait_ge(S["ln"], c_ln.v)
            HOP(VE.reduce_sum(stt["sum"][:], x2t, axis=AX))
            HOP(VE.tensor_scalar(stt["mu"][:], stt["sum"][:], 1.0 / D, None, ALU.mult))
            HOP(VE.tensor_scalar(stt["nmu"][:], stt["mu"][:], -1.0, None, ALU.mult))
            HOP(VE.tensor_mul(stt["t0"][:], stt["mu"][:], stt["mu"][:]))
            HOP(VE.tensor_scalar(stt["t1"][:], stt["sq"][:], 1.0 / D, None, ALU.mult))
            HOP(VE.tensor_sub(stt["ve"][:], stt["t1"][:], stt["t0"][:]))
            HOP(VE.tensor_scalar(stt["ve"][:], stt["ve"][:], 1e-5, None, ALU.add))

            c_ve2.inc(VE.engine_nop())
            SE.wait_ge(S["ve2"], c_ve2.v)
            c_ln.inc(SE.activation(stt["t0"][:], stt["ve"][:], ACT.Sqrt))
            VE.wait_ge(S["ln"], c_ln.v)
            HOP(VE.reciprocal(stt["rs"][:], stt["t0"][:]))
            HOP(VE.tensor_mul(stt["t0"][:], stt["rs"][:], stt["rs"][:]))
            HOP(VE.tensor_mul(stt["t1"][:], stt["t0"][:], stt["ve"][:]))
            HOP(VE.tensor_scalar(stt["t1"][:], stt["t1"][:], -0.5, 1.5, ALU.mult, ALU.add))
            HOP(VE.tensor_mul(stt["rs"][:], stt["rs"][:], stt["t1"][:]))
            HOP(VE.tensor_mul(stt["rss"][:], stt["rs"][:], scl[:]))

            c_ve2.inc(VE.engine_nop())
            VE.wait_ge(c_t5[(tt + 1) % 2].sem, c_t5[(tt + 1) % 2].v)
            c_lnw.inc(VE.tensor_scalar(q1tm8[:], x2t, stt["nmu"][:], stt["rss"][:],
                                       ALU.add, ALU.mult))
            VE.wait_ge(S["lnw"], c_lnw.v)
            c_lnw.inc(VE.tensor_copy(q1tmb[:], q1tm8[:]))
            SE.wait_ge(S["lnw"], c_lnw.v)
            if tt == 0:
                SE.wait_ge(S["pe4"], c_pe4.v)
            for c in range(KC):
                c_t5[tt % 2].inc(SE.dma_start_transpose(
                    q1_sl(c, tt * 128, (tt + 1) * 128),
                    q1tmb[:, c * 128:(c + 1) * 128]))
            TE.wait_ge(c_t5[tt % 2].sem, c_t5[tt % 2].v)
            TE.wait_ge(S["q2"], KC)
            for nb in range(4):
                par = ft % 2
                if ft >= 2:
                    TE.wait_ge(S["ve5"], ft - 1)
                for k in range(KC):
                    mm = TE.matmul(ps_f[par][:],
                                   q1_sl(k, tt * 128, (tt + 1) * 128),
                                   q2_f8[:, k * D + nb * 512: k * D + (nb + 1) * 512],
                                   start=(k == 0), stop=(k == KC - 1))
                c_pe5.inc(mm)
                VE.wait_ge(S["pe5"], c_pe5.v)
                if tt > 0:
                    VE.wait_ge(S["st"], 16 * tt)
                c_ve5.inc(VE.scalar_tensor_tensor(
                    ost[:, nb * 512:(nb + 1) * 512], ps_f[par][:], rr1[:],
                    x2t[:, nb * 512:(nb + 1) * 512], ALU.mult, ALU.add))
                ft += 1
            SY.wait_ge(S["ve5"], c_ve5.v)
            c_st.inc(SY.dma_start(out=y_h[tt * 128:(tt + 1) * 128, :], in_=ost[:]))
        SY.wait_ge(S["st"], c_st.v)

    ctx.pop_all()
    return nc


_NC_CACHE = []


def _rope_tables():
    freqs = np.exp(np.float32(-math.log(10000.0)) *
                   (np.arange(0, 128, 2, dtype=np.float32) /
                    np.float32(128.0))).astype(np.float32)
    pos = np.arange(T, dtype=np.float32)
    ang = (freqs[:, None] * pos[None, :]).astype(np.float32)
    sin = np.sin(ang).astype(np.float32)
    cos = np.cos(ang).astype(np.float32)
    cos_full = np.concatenate([cos, cos], axis=0)
    sin_signed = np.concatenate([-sin, sin], axis=0)
    return cos_full, sin_signed


def _perm_cols():
    r = np.arange(128)
    within = np.where(r < 64, 2 * r, 2 * (r - 64) + 1)
    return np.concatenate([h * 128 + within for h in range(H)])


def kernel(L0_attn_x, L0_attn_Wqkv, L0_attn_WO, L0_ffn_input2,
           L0_ffn_scale, L0_ffn_scale1):
    x = np.ascontiguousarray(np.asarray(L0_attn_x, dtype=np.float32))
    wqkv = np.asarray(L0_attn_Wqkv, dtype=np.float32)
    wo = np.asarray(L0_attn_WO, dtype=np.float32)
    inp2 = np.ascontiguousarray(np.asarray(L0_ffn_input2, dtype=np.float32))
    s = np.float32(np.asarray(L0_ffn_scale))
    s1 = np.float32(np.asarray(L0_ffn_scale1))

    wqkvT = np.ascontiguousarray(wqkv.T)
    perm = _perm_cols()
    wqkvT_p = wqkvT.copy()
    wqkvT_p[:, :D] = wqkvT[:, :D][:, perm]
    wqkvT_p[:, D:2 * D] = wqkvT[:, D:2 * D][:, perm]
    wqkvT_p = np.ascontiguousarray(wqkvT_p)
    woT = np.ascontiguousarray(wo.T)

    cos_full, sin_signed = _rope_tables()

    kk = np.arange(128)[:, None]
    qq = np.arange(512)[None, :]
    masks = np.concatenate(
        [(qq >= (d * 128 + kk)).astype(ml_dtypes.bfloat16) for d in range(4)],
        axis=1)
    onesb = np.ones((128, 128), dtype=ml_dtypes.bfloat16)
    onesf = np.ones((1, 128), dtype=np.float32)
    scl_b = np.full((128, 1), s, dtype=np.float32)
    scl1_b = np.full((128, 1), s1, dtype=np.float32)

    if not _NC_CACHE:
        _NC_CACHE.append(build_nc())
    nc = _NC_CACHE[0]

    in_maps = []
    for c in range(8):
        b, j = c // 4, c % 4
        xo = np.concatenate([x[b, j * 512:(j + 1) * 512],
                             x[b, (7 - j) * 512:(8 - j) * 512]], axis=0)
        gpos = np.concatenate([np.arange(j * 512, (j + 1) * 512),
                               np.arange((7 - j) * 512, (8 - j) * 512)])
        in_maps.append({
            "x_own": np.ascontiguousarray(xo),
            "wqkvT": wqkvT_p,
            "woT": woT,
            "inp2": inp2,
            "cosT": np.ascontiguousarray(cos_full[:, gpos]),
            "sinT": np.ascontiguousarray(sin_signed[:, gpos]),
            "maskT": masks,
            "onesb": onesb,
            "onesf": onesf,
            "scale_b": scl_b,
            "scale1_b": scl1_b,
        })

    res = run_bass_kernel_spmd(nc, in_maps, core_ids=list(range(8)))

    y = np.empty((B, T, D), dtype=np.float32)
    for c in range(8):
        b, j = c // 4, c % 4
        yo = res.results[c]["y_own"]
        y[b, j * 512:(j + 1) * 512] = yo[:512]
        y[b, (7 - j) * 512:(8 - j) * 512] = yo[512:]
    return y


# version 08
# speedup vs baseline: 1.4645x; 1.4645x over previous
import math
from contextlib import ExitStack

import numpy as np
import ml_dtypes
import concourse.bass as bass
import concourse.mybir as mybir
from concourse.bass_utils import run_bass_kernel_spmd
from concourse.expressions import RuntimeValue

F32 = mybir.dt.float32
BF16 = mybir.dt.bfloat16
FP8 = mybir.dt.float8e4
ALU = mybir.AluOpType
ACT = mybir.ActivationFunctionType
AX = mybir.AxisListType.X

D = 2048
H = 16
DH = 128
B, T = 2, 4096
NT = 1024
NTT = NT // 128
KC = D // 128
PH = DH * NT
KV_HALF = H * PH
EXP_BIAS = -12.0
SCALE_S = 1.0 / math.sqrt(DH)
RGROUPS = [[0, 1, 2, 3], [4, 5, 6, 7]]


class Ctr:
    def __init__(self, sem, step=1):
        self.sem, self.v, self.step = sem, 0, step

    def inc(self, inst, n=None):
        n = self.step if n is None else n
        inst.then_inc(self.sem, n)
        self.v += n
        return self.v


def build_nc():
    nc = bass.Bass("TRN2", target_bir_lowering=False)

    x_h = nc.dram_tensor("x_own", [NT, D], F32, kind="ExternalInput").ap()
    wqkv_h = nc.dram_tensor("wqkvT", [D, 3 * D], F32, kind="ExternalInput").ap()
    wo_h = nc.dram_tensor("woT", [D, D], F32, kind="ExternalInput").ap()
    inp2_h = nc.dram_tensor("inp2", [D, D], F32, kind="ExternalInput").ap()
    cos_h = nc.dram_tensor("cosT", [128, NT], F32, kind="ExternalInput").ap()
    sin_h = nc.dram_tensor("sinT", [128, NT], F32, kind="ExternalInput").ap()
    mask_h = nc.dram_tensor("maskT", [128, 4 * 512], BF16, kind="ExternalInput").ap()
    ones_h = nc.dram_tensor("onesb", [128, 128], BF16, kind="ExternalInput").ap()
    onesf_h = nc.dram_tensor("onesf", [1, 128], F32, kind="ExternalInput").ap()
    scl_h = nc.dram_tensor("scale_b", [128, 1], F32, kind="ExternalInput").ap()
    scl1_h = nc.dram_tensor("scale1_b", [128, 1], F32, kind="ExternalInput").ap()
    y_h = nc.dram_tensor("y_own", [NT, D], F32, kind="ExternalOutput").ap()

    kv_own = nc.dram_tensor("kv_own", [2 * KV_HALF], BF16, kind="Internal").ap()
    kv_all = nc.dram_tensor("kv_all", [4, 2 * KV_HALF], BF16, kind="Internal").ap()

    ctx = ExitStack()
    E = ctx.enter_context

    cos_t = E(nc.sbuf_tensor("cos_t", [128, NT], F32))
    sin_t = E(nc.sbuf_tensor("sin_t", [128, NT], F32))
    masks = E(nc.sbuf_tensor("masks_t", [128, 4 * 512], BF16))
    scl = E(nc.sbuf_tensor("scl_t", [128, 1], F32))
    scl1 = E(nc.sbuf_tensor("scl1_t", [128, 1], F32))
    onesb = E(nc.sbuf_tensor("onesb_t", [128, 128], BF16))
    onesf = E(nc.sbuf_tensor("onesf_t", [1, 128], F32))
    nbias = E(nc.sbuf_tensor("nbias_t", [128, 1], F32))
    wbuf = [E(nc.sbuf_tensor(f"wbuf{i}", [128, KC * 512], BF16)) for i in range(2)]
    q_fm = E(nc.sbuf_tensor("q_fm", [128, H * NT], BF16))
    av0 = E(nc.sbuf_tensor("av0", [128, NT], BF16))
    expT = [E(nc.sbuf_tensor(f"expT{i}", [128, 512], BF16)) for i in range(3)]
    kst = [E(nc.sbuf_tensor(f"kst{i}", [128, NT], BF16)) for i in range(2)]
    vst = [E(nc.sbuf_tensor(f"vst{i}", [128, 512], BF16)) for i in range(2)]
    bc_sb = E(nc.sbuf_tensor("bc_sb", [128, 512], F32))
    sums_sb = E(nc.sbuf_tensor("sums_sb", [1, 512], F32))
    recip_sb = E(nc.sbuf_tensor("recip_sb", [1, 512], F32))
    stt = {n: E(nc.sbuf_tensor(f"st_{n}", [128, 1], F32))
           for n in ["sum", "sq", "mu", "nmu", "ve", "t0", "t1", "rs", "rss"]}
    rr1 = E(nc.sbuf_tensor("rr1", [128, 1], F32))
    s8_sum = E(nc.sbuf_tensor("s8_sum", [128, NTT], F32))
    s8_sq = E(nc.sbuf_tensor("s8_sq", [128, NTT], F32))
    s8_a = E(nc.sbuf_tensor("s8_a", [128, NTT], F32))
    s8_b = E(nc.sbuf_tensor("s8_b", [128, NTT], F32))
    s8_nmu = E(nc.sbuf_tensor("s8_nmu", [128, NTT], F32))
    s8_rss = E(nc.sbuf_tensor("s8_rss", [128, NTT], F32))

    AR = 110 * 1024
    arena = nc.alloc_sbuf_tensor("arena", [128, AR], mybir.dt.uint8)
    abase = nc.lookup_mloc(arena).addr

    def at(name, off_kb, shape, dtype):
        return nc.alloc_sbuf_tensor_at(name, shape, dtype,
                                       offset=abase + off_kb * 1024)

    h1_fm = at("h1_fm", 0, [128, KC * NT], BF16)
    xt = [at(f"xt{i}", 32 + 8 * i, [128, D], F32) for i in range(2)]
    sq_sc = at("sq_sc", 48, [128, D], F32)
    h1tm = [at(f"h1tm{i}", 56 + 4 * i, [128, D], BF16) for i in range(2)]
    rp_c = at("rp_c", 32, [128, NT], F32)
    rp_s = at("rp_s", 36, [128, NT], F32)
    kbuf = [at(f"kbuf{i}", 8 * i, [128, 4096], BF16) for i in range(2)]
    vbuf = [at(f"vbuf{i}", 16 + 8 * i, [128, 4096], BF16) for i in range(2)]
    q2_f8 = at("q2_f8", 0, [128, KC * D], FP8)
    x2_sb = at("x2_sb", 32, [128, NTT * D], F32)
    i2st = at("i2st", 96, [128, D], F32)
    q1tm8 = at("q1tm8", 96, [128, D], FP8)
    q1tmb = at("q1tmb", 98, [128, D], BF16)
    ost = at("ost", 102, [128, D], F32)

    S = {n: nc.alloc_semaphore(n) for n in [
        "x", "tr", "wg", "pe2", "ve2", "kvw", "cc", "kvl", "as_", "ax", "msk",
        "avs", "smv", "ve3", "bcc", "bc", "nrm", "ph", "pe4", "ve4", "i2l",
        "q2", "pe5", "ve5", "tr5", "ln", "lnw", "st", "xr", "veh",
        "sx0", "sx1", "wg0", "wg1", "kw0", "kw1", "vw0", "vw1", "kl0", "kl1",
        "t50", "t51", "tr0", "tr1"]}

    c_x = Ctr(S["x"], 16)
    c_sx = [Ctr(S["sx0"], 16), Ctr(S["sx1"], 16)]
    c_tr = [Ctr(S["tr0"], 16), Ctr(S["tr1"], 16)]
    c_wg = [Ctr(S["wg0"], 16), Ctr(S["wg1"], 16)]
    c_kw = [Ctr(S["kw0"], 16), Ctr(S["kw1"], 16)]
    c_vw = [Ctr(S["vw0"], 16), Ctr(S["vw1"], 16)]
    c_t5 = [Ctr(S["t50"], 16), Ctr(S["t51"], 16)]
    c_pe2 = Ctr(S["pe2"])
    c_ve2 = Ctr(S["ve2"])
    c_kvw = Ctr(S["kvw"], 16)
    c_cc = Ctr(S["cc"])
    c_ln = Ctr(S["ln"])
    c_lnw = Ctr(S["lnw"])
    c_veh = Ctr(S["veh"])

    def q_sl(h, a, b):
        return q_fm[:, h * NT + a: h * NT + b]

    def av_sl(h, a, b):
        if h == 0:
            return av0[:, a:b]
        return q_fm[:, (h - 1) * NT + a: (h - 1) * NT + b]

    def q1_sl(c, a, b):
        w = wbuf[c // 8]
        cc_ = c % 8
        return w[:, cc_ * NT + a: cc_ * NT + b]

    TE, VE, SE, SY, GP = nc.tensor, nc.vector, nc.scalar, nc.sync, nc.gpsimd

    def HOP(inst):
        c_veh.inc(inst)
        VE.wait_ge(S["veh"], c_veh.v)
        return inst

    for dst, src in [(cos_t, cos_h), (sin_t, sin_h), (masks, mask_h),
                     (onesb, ones_h), (onesf, onesf_h),
                     (scl, scl_h), (scl1, scl1_h)]:
        c_x.inc(SY.dma_start(out=dst[:], in_=src[:]))
    CONST_LOADS = c_x.v

    VE.wait_ge(S["x"], CONST_LOADS)
    VE.memset(nbias[:], EXP_BIAS)
    HOP(VE.tensor_mul(stt["t0"][:], scl[:], scl1[:]))
    c_ve2.inc(VE.reciprocal(rr1[:], stt["t0"][:]))

    for tt in range(NTT):
        xcur = xt[tt % 2]
        if tt >= 2:
            SY.wait_ge(S["lnw"], tt - 1)
        c_sx[tt % 2].inc(SY.dma_start(out=xcur[:], in_=x_h[tt * 128:(tt + 1) * 128, :]))

        SE.wait_ge(S["x"], CONST_LOADS)
        SE.wait_ge(c_sx[tt % 2].sem, c_sx[tt % 2].v)
        if tt > 0:
            SE.wait_ge(S["lnw"], tt)
        c_ln.inc(SE.activation(sq_sc[:], xcur[:], ACT.Square, accum_out=stt["sq"][:]))

        VE.wait_ge(c_sx[tt % 2].sem, c_sx[tt % 2].v)
        VE.wait_ge(S["ln"], c_ln.v)
        HOP(VE.reduce_sum(stt["sum"][:], xcur[:], axis=AX))
        HOP(VE.tensor_scalar(stt["mu"][:], stt["sum"][:], 1.0 / D, None, ALU.mult))
        HOP(VE.tensor_scalar(stt["nmu"][:], stt["mu"][:], -1.0, None, ALU.mult))
        HOP(VE.tensor_mul(stt["t0"][:], stt["mu"][:], stt["mu"][:]))
        HOP(VE.tensor_scalar(stt["t1"][:], stt["sq"][:], 1.0 / D, None, ALU.mult))
        HOP(VE.tensor_sub(stt["ve"][:], stt["t1"][:], stt["t0"][:]))
        HOP(VE.tensor_scalar(stt["ve"][:], stt["ve"][:], 1e-5, None, ALU.add))

        c_ve2.inc(VE.engine_nop())
        SE.wait_ge(S["ve2"], c_ve2.v)
        c_ln.inc(SE.activation(stt["t0"][:], stt["ve"][:], ACT.Sqrt))
        VE.wait_ge(S["ln"], c_ln.v)
        HOP(VE.reciprocal(stt["rs"][:], stt["t0"][:]))
        HOP(VE.tensor_mul(stt["t0"][:], stt["rs"][:], stt["rs"][:]))
        HOP(VE.tensor_mul(stt["t1"][:], stt["t0"][:], stt["ve"][:]))
        HOP(VE.tensor_scalar(stt["t1"][:], stt["t1"][:], -0.5, 1.5, ALU.mult, ALU.add))
        HOP(VE.tensor_mul(stt["rs"][:], stt["rs"][:], stt["t1"][:]))

        c_ve2.inc(VE.engine_nop())
        VE.wait_ge(c_tr[tt % 2].sem, c_tr[tt % 2].v)
        c_lnw.inc(VE.tensor_scalar(h1tm[tt % 2][:], xcur[:], stt["nmu"][:],
                                   stt["rs"][:], ALU.add, ALU.mult))
        SE.wait_ge(S["lnw"], c_lnw.v)
        for c in range(KC):
            c_tr[tt % 2].inc(SE.dma_start_transpose(
                h1_fm[:, c * NT + tt * 128: c * NT + (tt + 1) * 128],
                h1tm[tt % 2][:, c * 128:(c + 1) * 128]))
    H1_TR = (c_tr[0].v, c_tr[1].v)

    OGS = [4, 5, 6, 7, 8, 9, 10, 11, 0, 1, 2, 3]

    def emit_wload(src_ap, og, buf):
        for kg in range(4):
            src = src_ap[kg * 512:(kg + 1) * 512, og * 512:(og + 1) * 512] \
                .rearrange("(k p) c -> p k c", p=128)
            dst = wbuf[buf][:, kg * 4 * 512:(kg + 1) * 4 * 512] \
                .rearrange("p (k c) -> p k c", k=4)
            c_wg[buf].inc(GP.dma_start(out=dst, in_=src))

    emit_wload(wqkv_h, OGS[0], 0)
    emit_wload(wqkv_h, OGS[1], 1)

    with nc.psum_tensor("ps_qk0", [128, NT], F32) as pqk0, \
         nc.psum_tensor("ps_qk1", [128, NT], F32) as pqk1, \
         nc.psum_tensor("ps_v0", [128, 512], F32) as pv0, \
         nc.psum_tensor("ps_v1", [128, 512], F32) as pv1:
        pqk = [pqk0, pqk1]
        pv = [pv0, pv1]
        n_kvw = 0
        n_kvw_at_ag = [48]
        last_qk_evac = [0, 0]
        last_rope_add = 0
        last_v_evac = [0, 0]
        last_kst_dma = [0, 0]
        last_vst_dma = [0, 0]
        qk_i = 0
        v_i = 0

        TE.wait_ge(c_tr[0].sem, H1_TR[0])
        TE.wait_ge(c_tr[1].sem, H1_TR[1])

        for ogi, og in enumerate(OGS):
            buf = ogi % 2
            TE.wait_ge(c_wg[buf].sem, 16 * 4 * (ogi // 2 + 1))
            if 8 <= og <= 11:
                vhb = (og - 8) * 4
                vsec = kv_own[KV_HALF + vhb * PH: KV_HALF + (vhb + 4) * PH] \
                    .rearrange("(h t d) -> h t d", h=4, d=128)
                for tt in range(NTT):
                    par = v_i % 2
                    if last_v_evac[par]:
                        TE.wait_ge(S["ve2"], last_v_evac[par])
                    for k in range(KC):
                        mm = TE.matmul(
                            pv[par][:],
                            h1_fm[:, k * NT + tt * 128: k * NT + (tt + 1) * 128],
                            wbuf[buf][:, k * 512:(k + 1) * 512],
                            start=(k == 0), stop=(k == KC - 1))
                    c_pe2.inc(mm)
                    VE.wait_ge(S["pe2"], c_pe2.v)
                    sp = tt % 2
                    if last_vst_dma[sp]:
                        VE.wait_ge(c_vw[sp].sem, last_vst_dma[sp])
                    last_v_evac[par] = c_ve2.inc(VE.tensor_copy(vst[sp][:], pv[par][:]))
                    SY.wait_ge(S["ve2"], c_ve2.v)
                    c_vw[sp].inc(SY.dma_start(
                        out=vsec[:, tt * 128:(tt + 1) * 128, :].transpose([1, 0, 2]),
                        in_=vst[sp][:].rearrange("p (h d) -> p h d", h=4)))
                    last_vst_dma[sp] = c_vw[sp].v
                    n_kvw += 1
                    v_i += 1
            else:
                for jj in range(4):
                    par = qk_i % 2
                    if last_qk_evac[par]:
                        TE.wait_ge(S["ve2"], last_qk_evac[par])
                    for tb in range(2):
                        for k in range(KC):
                            mm = TE.matmul(
                                pqk[par][:, tb * 512:(tb + 1) * 512],
                                wbuf[buf][:, k * 512 + jj * 128: k * 512 + jj * 128 + 128],
                                h1_fm[:, k * NT + tb * 512: k * NT + (tb + 1) * 512],
                                start=(k == 0), stop=(k == KC - 1))
                        c_pe2.inc(mm)
                    VE.wait_ge(S["pe2"], c_pe2.v)
                    src = pqk[par]
                    if last_rope_add:
                        VE.wait_ge(S["ve2"], last_rope_add)
                    VE.tensor_mul(rp_c[:], src[:], cos_t[:])
                    VE.tensor_mul(rp_s[0:64, :], src[64:128, :], sin_t[0:64, :])
                    HOP(VE.tensor_mul(rp_s[64:128, :], src[0:64, :], sin_t[64:128, :]))
                    if og >= 4:
                        h = (og - 4) * 4 + jj
                        sp = h % 2
                        if last_kst_dma[sp]:
                            VE.wait_ge(c_kw[sp].sem, last_kst_dma[sp])
                        last_qk_evac[par] = c_ve2.inc(
                            VE.tensor_add(kst[sp][:], rp_c[:], rp_s[:]))
                        last_rope_add = c_ve2.v
                        SY.wait_ge(S["ve2"], c_ve2.v)
                        c_kw[sp].inc(SY.dma_start(
                            out=kv_own[h * PH: (h + 1) * PH]
                            .rearrange("(d t) -> d t", d=128),
                            in_=kst[sp][:]))
                        last_kst_dma[sp] = c_kw[sp].v
                        n_kvw += 1
                    else:
                        h = og * 4 + jj
                        last_qk_evac[par] = c_ve2.inc(
                            VE.tensor_add(q_sl(h, 0, NT), rp_c[:], rp_s[:]))
                        last_rope_add = c_ve2.v
                    qk_i += 1
            if ogi + 2 < len(OGS):
                GP.wait_ge(S["pe2"], c_pe2.v)
                emit_wload(wqkv_h, OGS[ogi + 2], buf)
            if ogi == 9:
                GP.wait_ge(c_kw[0].sem, c_kw[0].v)
                GP.wait_ge(c_kw[1].sem, c_kw[1].v)
                GP.wait_ge(c_vw[0].sem, c_vw[0].v)
                GP.wait_ge(c_vw[1].sem, c_vw[1].v)
                c_cc.inc(GP.collective_compute(
                    "AllGather", ALU.bypass, replica_groups=RGROUPS,
                    ins=[kv_own[:]], outs=[kv_all[:]]))

    QKV_VE = c_ve2.v
    QKV_PE = c_pe2.v

    engines = [TE, VE, SE, SY, GP]
    pid = nc.partition_id()
    idx = []
    for eng in engines:
        reg = eng.alloc_register(f"jreg_{eng.engine.value}")
        eng.reg_mov(reg, pid)
        eng.reg_alu(reg, reg, 4, ALU.mod)
        idx.append(RuntimeValue(reg, min_val=0, max_val=3))

    def emit_wo(db, buf):
        for kg in range(4):
            src = wo_h[kg * 512:(kg + 1) * 512, db * 512:(db + 1) * 512] \
                .rearrange("(k p) c -> p k c", p=128)
            dst = wbuf[buf][:, kg * 4 * 512:(kg + 1) * 4 * 512] \
                .rearrange("p (k c) -> p k c", k=4)
            c_wg[buf].inc(GP.dma_start(out=dst, in_=src))

    WO_B = (c_wg[0].v, c_wg[1].v)

    with nc.psum_tensor("ps_s0", [128, 512], F32) as ps_s0, \
         nc.psum_tensor("ps_s1", [128, 512], F32) as ps_s1, \
         nc.psum_tensor("ps_a0", [128, 512], F32) as ps_a0, \
         nc.psum_tensor("ps_a1", [128, 512], F32) as ps_a1, \
         nc.psum_tensor("ps_m0", [1, 512], F32) as ps_m0, \
         nc.psum_tensor("ps_m1", [1, 512], F32) as ps_m1, \
         nc.psum_tensor("ps_bc", [128, 512], F32) as ps_bc:
        ps_s = [ps_s0, ps_s1]
        ps_a = [ps_a0, ps_a1]
        ps_m = [ps_m0, ps_m1]

        VEH_BASE = c_veh.v
        veh_body_counts = []
        for j in nc.Switch(engines=engines, index=idx, n=4):
            a_veh = Ctr(S["veh"])
            a_veh.v = VEH_BASE

            def HOPB(inst):
                a_veh.inc(inst)
                VE.wait_ge(S["veh"], a_veh.v)
                return inst

            a_kvl = [Ctr(S["kl0"], 16), Ctr(S["kl1"], 16)]
            a_as = Ctr(S["as_"])
            a_ax = Ctr(S["ax"])
            a_msk = Ctr(S["msk"])
            a_avs = Ctr(S["avs"])
            a_smv = Ctr(S["smv"])
            a_ve3 = Ctr(S["ve3"])
            a_bcc = Ctr(S["bcc"])
            a_bc = Ctr(S["bc"])
            a_nrm = Ctr(S["nrm"])
            a_xr = Ctr(S["xr"], 16)

            NBLK = 8 - j
            blocks = [j, 7 - j]
            lph = 2 * NBLK
            av_of_pair = {}
            pend_bcast = []

            def emit_kv_loads(h):
                hb = h % 2
                for g in range(NBLK):
                    r = g if g < 4 else 7 - g
                    half = 0 if g < 4 else 1
                    ksec = kv_all[r, h * PH: (h + 1) * PH] \
                        .rearrange("(d t) -> d t", d=128)
                    a_kvl[hb].inc(SY.dma_start(
                        out=kbuf[hb][:, g * 512:(g + 1) * 512],
                        in_=ksec[:, half * 512:(half + 1) * 512]))
                    vsec = kv_all[r, KV_HALF + h * PH: KV_HALF + (h + 1) * PH] \
                        .rearrange("(t d) -> t d", d=128)
                    a_kvl[hb].inc(SY.dma_start(
                        out=vbuf[hb][:, g * 512:(g + 1) * 512]
                        .rearrange("p (q d) -> p q d", q=4),
                        in_=vsec[half * 512:(half + 1) * 512, :]
                        .rearrange("(q p) d -> p q d", p=128)))

            def emit_bcast(p):
                h_, qb_, par_ = av_of_pair[p]
                TE.wait_ge(S["ve3"], p + 1)
                if p > 0:
                    TE.wait_ge(S["bcc"], p)
                a_bc.inc(TE.matmul(ps_bc[:], onesf[:, :], recip_sb[:, :],
                                   start=True, stop=True, skip_group_check=True))
                VE.wait_ge(S["bc"], a_bc.v)
                if a_nrm.v:
                    VE.wait_ge(S["nrm"], a_nrm.v)
                HOPB(VE.tensor_copy(bc_sb[:], ps_bc[:]))
                a_bcc.inc(VE.engine_nop())
                qoff = qb_ * 512
                a_nrm.inc(VE.tensor_mul(av_sl(h_, qoff, qoff + 512),
                                        ps_a[par_][:], bc_sb[:]))

            SY.wait_ge(S["cc"], 1)
            SY.wait_ge(S["pe2"], QKV_PE)
            SY.wait_ge(S["ve2"], QKV_VE)
            SE.wait_ge(S["ve2"], QKV_VE)
            emit_kv_loads(0)
            emit_kv_loads(1)

            gi = 0
            pairidx = 0
            xr_tt = 0
            pend_av = [None]

            def flush_av():
                if pend_av[0] is not None:
                    pend_av[0]()
                    pend_av[0] = None

            for h in range(H):
                TE.wait_ge(a_kvl[h % 2].sem, 16 * lph * (h // 2 + 1))
                for qbi, jb in enumerate(blocks):
                    P = 4 * (jb + 1)
                    par = pairidx % 2
                    qoff = qbi * 512
                    while pend_bcast and pend_bcast[0] <= pairidx - 2:
                        emit_bcast(pend_bcast.pop(0))
                    first_gi = gi
                    for kt in range(P):
                        if gi >= 2:
                            TE.wait_ge(S["ax"], gi - 1)
                        a_as.inc(TE.matmul(ps_s[gi % 2][:],
                                           kbuf[h % 2][:, kt * 128:(kt + 1) * 128],
                                           q_sl(h, qoff, qoff + 512),
                                           start=True, stop=True,
                                           skip_group_check=True))
                        SE.wait_ge(S["as_"], a_as.v)
                        if gi >= 3:
                            SE.wait_ge(S["avs"], 2 * (gi - 2))
                        a_ax.inc(SE.activation(expT[gi % 3][:], ps_s[gi % 2][:],
                                               ACT.Exp, bias=nbias[:], scale=SCALE_S))
                        diag = kt >= P - 4
                        if diag:
                            dd = kt - (P - 4)
                            VE.wait_ge(S["ax"], a_ax.v)
                            a_msk.inc(VE.tensor_mul(
                                expT[gi % 3][:], expT[gi % 3][:],
                                masks[:, dd * 512:(dd + 1) * 512]))
                        flush_av()
                        a_avs.v += 2

                        def em(gi_=gi, par_=par, kt_=kt, P_=P, h_=h,
                               pairidx_=pairidx, diag_=diag,
                               ax_v=a_ax.v, msk_v=a_msk.v):
                            if diag_:
                                TE.wait_ge(S["msk"], msk_v)
                            else:
                                TE.wait_ge(S["ax"], ax_v)
                            if kt_ == 0 and pairidx_ >= 2:
                                TE.wait_ge(S["nrm"], pairidx_ - 1)
                            TE.matmul(ps_a[par_][:],
                                      vbuf[h_ % 2][:, kt_ * 128:(kt_ + 1) * 128],
                                      expT[gi_ % 3][:],
                                      start=(kt_ == 0), stop=(kt_ == P_ - 1),
                                      skip_group_check=True) \
                                .then_inc(S["avs"], 1)
                            if kt_ == 0 and pairidx_ >= 2:
                                TE.wait_ge(S["smv"], pairidx_ - 1)
                            TE.matmul(ps_m[par_][:], onesb[:, 0:1],
                                      expT[gi_ % 3][:],
                                      start=(kt_ == 0), stop=(kt_ == P_ - 1),
                                      skip_group_check=True) \
                                .then_inc(S["avs"], 1)
                        pend_av[0] = em
                        gi += 1
                        if pend_bcast and (gi - first_gi) == 8:
                            emit_bcast(pend_bcast.pop(0))
                    while pend_bcast and pend_bcast[0] <= pairidx - 1:
                        emit_bcast(pend_bcast.pop(0))
                    SE.wait_ge(S["avs"], a_avs.v)
                    if a_ve3.v:
                        SE.wait_ge(S["ve3"], a_ve3.v)
                        SE.wait_ge(S["bc"], a_bc.v)
                    a_smv.inc(SE.activation(sums_sb[:], ps_m[par][:], ACT.Ln))
                    SE.wait_ge(S["smv"], a_smv.v)
                    a_ve3.inc(SE.activation(recip_sb[:], sums_sb[:], ACT.Exp,
                                            scale=-1.0))
                    av_of_pair[pairidx] = (h, qbi, par)
                    pend_bcast.append(pairidx)
                    pairidx += 1
                if h + 2 < H:
                    SY.wait_ge(S["avs"], a_avs.v)
                    emit_kv_loads(h + 2)
                if xr_tt < NTT:
                    a_xr.inc(SY.dma_start(
                        out=x2_sb[:, xr_tt * D:(xr_tt + 1) * D],
                        in_=x_h[xr_tt * 128:(xr_tt + 1) * 128, :]))
                    xr_tt += 1
            flush_av()
            while pend_bcast:
                emit_bcast(pend_bcast.pop(0))

            TE.wait_ge(S["bc"], a_bc.v)
            TE.sem_inc(S["ph"], 1)
            VE.wait_ge(S["nrm"], a_nrm.v)
            VE.sem_inc(S["ph"], 1)
            SE.wait_ge(S["ax"], a_ax.v)
            SE.sem_inc(S["ph"], 1)
            SY.wait_ge(a_kvl[0].sem, a_kvl[0].v)
            SY.wait_ge(a_kvl[1].sem, a_kvl[1].v)
            SY.wait_ge(S["xr"], a_xr.v)
            SY.sem_inc(S["ph"], 1)
            GP.wait_ge(S["cc"], 1)
            GP.wait_ge(S["pe2"], QKV_PE)
            c_wg[0].v, c_wg[1].v = WO_B
            emit_wo(0, 0)
            emit_wo(1, 1)
            GP.sem_inc(S["ph"], 1)
            veh_body_counts.append(a_veh.v - VEH_BASE)

    assert len(set(veh_body_counts)) == 1, veh_body_counts
    c_veh.v = VEH_BASE + veh_body_counts[0]
    c_wg[0].v = WO_B[0] + 64
    c_wg[1].v = WO_B[1] + 64

    c_pe4 = Ctr(S["pe4"])
    c_ve4 = Ctr(S["ve4"])
    c_i2l = Ctr(S["i2l"], 16)
    c_q2 = Ctr(S["q2"])

    for eng in engines:
        eng.wait_ge(S["ph"], 5)

    WG0 = [WO_B[0], WO_B[1]]

    def emit_i2(k):
        SE.wait_ge(S["q2"], k)
        c_i2l.inc(SE.dma_start(out=i2st[:], in_=inp2_h[k * 128:(k + 1) * 128, :]))
        VE.wait_ge(S["i2l"], c_i2l.v)
        c_q2.inc(VE.tensor_scalar(q2_f8[:, k * D:(k + 1) * D], i2st[:],
                                  scl1[:], None, ALU.mult))

    next_k = [0]

    def maybe_i2():
        if next_k[0] < KC:
            emit_i2(next_k[0])
            next_k[0] += 1

    with nc.psum_tensor("ps_o0", [128, 512], F32) as ps_o0, \
         nc.psum_tensor("ps_o1", [128, 512], F32) as ps_o1:
        ps_o = [ps_o0, ps_o1]
        ot = 0
        for db in range(4):
            TE.wait_ge(c_wg[db % 2].sem, WG0[db % 2] + 16 * 4 * (db // 2 + 1))
            for tt in range(NTT):
                par = ot % 2
                if ot >= 2:
                    TE.wait_ge(S["ve4"], ot - 1)
                for c in range(KC):
                    mm = TE.matmul(ps_o[par][:],
                                   av_sl(c, tt * 128, (tt + 1) * 128),
                                   wbuf[db % 2][:, c * 512:(c + 1) * 512],
                                   start=(c == 0), stop=(c == KC - 1))
                c_pe4.inc(mm)
                VE.wait_ge(S["pe4"], c_pe4.v)
                VE.wait_ge(S["xr"], 16 * NTT)
                xsl = x2_sb[:, tt * D + db * 512: tt * D + (db + 1) * 512]
                c_ve4.inc(VE.tensor_add(xsl, ps_o[par][:], xsl))
                ot += 1
                if ot % 2 == 0:
                    maybe_i2()
            if db + 2 < 4:
                GP.wait_ge(S["pe4"], c_pe4.v)
                emit_wo(db + 2, db % 2)
    while next_k[0] < KC:
        maybe_i2()

    c_pe5 = Ctr(S["pe5"])
    c_ve5 = Ctr(S["ve5"])
    c_tr5 = Ctr(S["tr5"], 16)
    c_st = Ctr(S["st"], 16)

    with nc.psum_tensor("ps_f0", [128, 512], F32) as ps_f0, \
         nc.psum_tensor("ps_f1", [128, 512], F32) as ps_f1, \
         nc.psum_tensor("ps_sq", [128, D], F32) as ps_sq:
        ps_f = [ps_f0, ps_f1]

        for tt in range(NTT):
            x2t = x2_sb[:, tt * D:(tt + 1) * D]
            SE.wait_ge(S["ve4"], 25 + tt)
            c_ln.inc(SE.activation(ps_sq[:], x2t, ACT.Square,
                                   accum_out=s8_sq[:, tt:tt + 1]))
            SE.wait_ge(S["ln"], c_ln.v)
            VE.wait_ge(S["ve4"], 25 + tt)
            HOP(VE.reduce_sum(s8_sum[:, tt:tt + 1], x2t, axis=AX))
        VE.wait_ge(S["ln"], c_ln.v)
        HOP(VE.tensor_scalar(s8_a[:], s8_sum[:], 1.0 / D, None, ALU.mult))
        HOP(VE.tensor_scalar(s8_nmu[:], s8_a[:], -1.0, None, ALU.mult))
        HOP(VE.tensor_mul(s8_b[:], s8_a[:], s8_a[:]))
        HOP(VE.tensor_scalar(s8_a[:], s8_sq[:], 1.0 / D, None, ALU.mult))
        HOP(VE.tensor_sub(s8_a[:], s8_a[:], s8_b[:]))
        HOP(VE.tensor_scalar(s8_a[:], s8_a[:], 1e-5, None, ALU.add))
        c_ve2.inc(VE.engine_nop())
        SE.wait_ge(S["ve2"], c_ve2.v)
        c_ln.inc(SE.activation(s8_b[:], s8_a[:], ACT.Ln))
        SE.wait_ge(S["ln"], c_ln.v)
        c_ln.inc(SE.activation(s8_a[:], s8_b[:], ACT.Exp, scale=-0.5))
        VE.wait_ge(S["ln"], c_ln.v)
        HOP(VE.tensor_scalar(s8_rss[:], s8_a[:], scl[:], None, ALU.mult))

        ft = 0
        for tt in range(NTT):
            x2t = x2_sb[:, tt * D:(tt + 1) * D]
            VE.wait_ge(c_t5[(tt + 1) % 2].sem, c_t5[(tt + 1) % 2].v)
            c_lnw.inc(VE.tensor_scalar(q1tm8[:], x2t, s8_nmu[:, tt:tt + 1],
                                       s8_rss[:, tt:tt + 1], ALU.add, ALU.mult))
            VE.wait_ge(S["lnw"], c_lnw.v)
            c_lnw.inc(VE.tensor_copy(q1tmb[:], q1tm8[:]))
            SE.wait_ge(S["lnw"], c_lnw.v)
            SY.wait_ge(S["lnw"], c_lnw.v)
            if tt == 0:
                SE.wait_ge(S["pe4"], c_pe4.v)
                SY.wait_ge(S["pe4"], c_pe4.v)
            for c in range(KC):
                q = SE if c % 2 == 0 else SY
                c_t5[tt % 2].inc(q.dma_start_transpose(
                    q1_sl(c, tt * 128, (tt + 1) * 128),
                    q1tmb[:, c * 128:(c + 1) * 128]))
            TE.wait_ge(c_t5[tt % 2].sem, c_t5[tt % 2].v)
            if tt == 0:
                TE.wait_ge(S["q2"], KC)
            for nb in range(4):
                par = ft % 2
                if ft >= 2:
                    TE.wait_ge(S["ve5"], ft - 1)
                for k in range(KC):
                    mm = TE.matmul(ps_f[par][:],
                                   q1_sl(k, tt * 128, (tt + 1) * 128),
                                   q2_f8[:, k * D + nb * 512: k * D + (nb + 1) * 512],
                                   start=(k == 0), stop=(k == KC - 1))
                c_pe5.inc(mm)
                VE.wait_ge(S["pe5"], c_pe5.v)
                if tt > 0:
                    VE.wait_ge(S["st"], 16 * tt)
                c_ve5.inc(VE.scalar_tensor_tensor(
                    ost[:, nb * 512:(nb + 1) * 512], ps_f[par][:], rr1[:],
                    x2t[:, nb * 512:(nb + 1) * 512], ALU.mult, ALU.add))
                ft += 1
            SY.wait_ge(S["ve5"], c_ve5.v)
            c_st.inc(SY.dma_start(out=y_h[tt * 128:(tt + 1) * 128, :], in_=ost[:]))
        SY.wait_ge(S["st"], c_st.v)

    ctx.pop_all()
    return nc


_NC_CACHE = []


def _rope_tables():
    freqs = np.exp(np.float32(-math.log(10000.0)) *
                   (np.arange(0, 128, 2, dtype=np.float32) /
                    np.float32(128.0))).astype(np.float32)
    pos = np.arange(T, dtype=np.float32)
    ang = (freqs[:, None] * pos[None, :]).astype(np.float32)
    sin = np.sin(ang).astype(np.float32)
    cos = np.cos(ang).astype(np.float32)
    cos_full = np.concatenate([cos, cos], axis=0)
    sin_signed = np.concatenate([-sin, sin], axis=0)
    return cos_full, sin_signed


def _perm_cols():
    r = np.arange(128)
    within = np.where(r < 64, 2 * r, 2 * (r - 64) + 1)
    return np.concatenate([h * 128 + within for h in range(H)])


def kernel(L0_attn_x, L0_attn_Wqkv, L0_attn_WO, L0_ffn_input2,
           L0_ffn_scale, L0_ffn_scale1):
    x = np.ascontiguousarray(np.asarray(L0_attn_x, dtype=np.float32))
    wqkv = np.asarray(L0_attn_Wqkv, dtype=np.float32)
    wo = np.asarray(L0_attn_WO, dtype=np.float32)
    inp2 = np.ascontiguousarray(np.asarray(L0_ffn_input2, dtype=np.float32))
    s = np.float32(np.asarray(L0_ffn_scale))
    s1 = np.float32(np.asarray(L0_ffn_scale1))

    wqkvT = np.ascontiguousarray(wqkv.T)
    perm = _perm_cols()
    wqkvT_p = wqkvT.copy()
    wqkvT_p[:, :D] = wqkvT[:, :D][:, perm]
    wqkvT_p[:, D:2 * D] = wqkvT[:, D:2 * D][:, perm]
    wqkvT_p = np.ascontiguousarray(wqkvT_p)
    woT = np.ascontiguousarray(wo.T)

    cos_full, sin_signed = _rope_tables()

    kk = np.arange(128)[:, None]
    qq = np.arange(512)[None, :]
    masks = np.concatenate(
        [(qq >= (d * 128 + kk)).astype(ml_dtypes.bfloat16) for d in range(4)],
        axis=1)
    onesb = np.ones((128, 128), dtype=ml_dtypes.bfloat16)
    onesf = np.ones((1, 128), dtype=np.float32)
    scl_b = np.full((128, 1), s, dtype=np.float32)
    scl1_b = np.full((128, 1), s1, dtype=np.float32)

    if not _NC_CACHE:
        _NC_CACHE.append(build_nc())
    nc = _NC_CACHE[0]

    in_maps = []
    for c in range(8):
        b, j = c // 4, c % 4
        xo = np.concatenate([x[b, j * 512:(j + 1) * 512],
                             x[b, (7 - j) * 512:(8 - j) * 512]], axis=0)
        gpos = np.concatenate([np.arange(j * 512, (j + 1) * 512),
                               np.arange((7 - j) * 512, (8 - j) * 512)])
        in_maps.append({
            "x_own": np.ascontiguousarray(xo),
            "wqkvT": wqkvT_p,
            "woT": woT,
            "inp2": inp2,
            "cosT": np.ascontiguousarray(cos_full[:, gpos]),
            "sinT": np.ascontiguousarray(sin_signed[:, gpos]),
            "maskT": masks,
            "onesb": onesb,
            "onesf": onesf,
            "scale_b": scl_b,
            "scale1_b": scl1_b,
        })

    res = run_bass_kernel_spmd(nc, in_maps, core_ids=list(range(8)))

    y = np.empty((B, T, D), dtype=np.float32)
    for c in range(8):
        b, j = c // 4, c % 4
        yo = res.results[c]["y_own"]
        y[b, j * 512:(j + 1) * 512] = yo[:512]
        y[b, (7 - j) * 512:(8 - j) * 512] = yo[512:]
    return y


# version 13
# speedup vs baseline: 1.4989x; 1.0235x over previous
import math
from contextlib import ExitStack

import numpy as np
import ml_dtypes
import concourse.bass as bass
import concourse.mybir as mybir
from concourse.bass_utils import run_bass_kernel_spmd
from concourse.expressions import RuntimeValue

F32 = mybir.dt.float32
BF16 = mybir.dt.bfloat16
FP8 = mybir.dt.float8e4
ALU = mybir.AluOpType
ACT = mybir.ActivationFunctionType
AX = mybir.AxisListType.X

D = 2048
H = 16
DH = 128
B, T = 2, 4096
NT = 1024
NTT = NT // 128
KC = D // 128
PH = DH * NT
KV_HALF = H * PH
EXP_BIAS = -12.0
SCALE_S = 1.0 / math.sqrt(DH)
RGROUPS = [[0, 1, 2, 3], [4, 5, 6, 7]]


class Ctr:
    def __init__(self, sem, step=1):
        self.sem, self.v, self.step = sem, 0, step

    def inc(self, inst, n=None):
        n = self.step if n is None else n
        inst.then_inc(self.sem, n)
        self.v += n
        return self.v


def build_nc():
    nc = bass.Bass("TRN2", target_bir_lowering=False)

    x_h = nc.dram_tensor("x_own", [NT, D], F32, kind="ExternalInput").ap()
    wqkv_h = nc.dram_tensor("wqkvT", [D, 3 * D], F32, kind="ExternalInput").ap()
    wo_h = nc.dram_tensor("woT", [D, D], F32, kind="ExternalInput").ap()
    inp2_h = nc.dram_tensor("inp2", [D, D], F32, kind="ExternalInput").ap()
    cos_h = nc.dram_tensor("cosT", [128, NT], F32, kind="ExternalInput").ap()
    sin_h = nc.dram_tensor("sinT", [128, NT], F32, kind="ExternalInput").ap()
    mask_h = nc.dram_tensor("maskT", [128, 4 * 512], BF16, kind="ExternalInput").ap()
    ones_h = nc.dram_tensor("onesb", [128, 128], BF16, kind="ExternalInput").ap()
    onesf_h = nc.dram_tensor("onesf", [1, 128], F32, kind="ExternalInput").ap()
    scl_h = nc.dram_tensor("scale_b", [128, 1], F32, kind="ExternalInput").ap()
    scl1_h = nc.dram_tensor("scale1_b", [128, 1], F32, kind="ExternalInput").ap()
    y_h = nc.dram_tensor("y_own", [NT, D], F32, kind="ExternalOutput").ap()

    kv_own = nc.dram_tensor("kv_own", [2 * KV_HALF], BF16, kind="Internal").ap()
    kv_all = nc.dram_tensor("kv_all", [4, 2 * KV_HALF], BF16, kind="Internal").ap()

    ctx = ExitStack()
    E = ctx.enter_context

    cos_t = E(nc.sbuf_tensor("cos_t", [128, NT], F32))
    sin_t = E(nc.sbuf_tensor("sin_t", [128, NT], F32))
    masks = E(nc.sbuf_tensor("masks_t", [128, 4 * 512], BF16))
    scl = E(nc.sbuf_tensor("scl_t", [128, 1], F32))
    scl1 = E(nc.sbuf_tensor("scl1_t", [128, 1], F32))
    onesb = E(nc.sbuf_tensor("onesb_t", [128, 128], BF16))
    onesf = E(nc.sbuf_tensor("onesf_t", [1, 128], F32))
    nbias = E(nc.sbuf_tensor("nbias_t", [128, 1], F32))
    wbuf = [E(nc.sbuf_tensor(f"wbuf{i}", [128, KC * 512], BF16)) for i in range(2)]
    q_fm = E(nc.sbuf_tensor("q_fm", [128, H * NT], BF16))
    av0 = E(nc.sbuf_tensor("av0", [128, NT], BF16))
    expT = [E(nc.sbuf_tensor(f"expT{i}", [128, 512], BF16)) for i in range(3)]
    kst = [E(nc.sbuf_tensor(f"kst{i}", [128, NT], BF16)) for i in range(2)]
    vst = [E(nc.sbuf_tensor(f"vst{i}", [128, 512], BF16)) for i in range(2)]
    bc_sb = E(nc.sbuf_tensor("bc_sb", [128, 512], F32))
    sums_sb = E(nc.sbuf_tensor("sums_sb", [1, 512], F32))
    recip_sb = E(nc.sbuf_tensor("recip_sb", [1, 512], F32))
    stt = {n: E(nc.sbuf_tensor(f"st_{n}", [128, 1], F32))
           for n in ["sum", "sq", "mu", "nmu", "ve", "t0", "t1", "rs", "rss"]}
    rr1 = E(nc.sbuf_tensor("rr1", [128, 1], F32))
    s8_sum = E(nc.sbuf_tensor("s8_sum", [128, NTT], F32))
    s8_sq = E(nc.sbuf_tensor("s8_sq", [128, NTT], F32))
    s8_a = E(nc.sbuf_tensor("s8_a", [128, NTT], F32))
    s8_b = E(nc.sbuf_tensor("s8_b", [128, NTT], F32))
    s8_nmu = E(nc.sbuf_tensor("s8_nmu", [128, NTT], F32))
    s8_rss = E(nc.sbuf_tensor("s8_rss", [128, NTT], F32))

    AR = 110 * 1024
    arena = nc.alloc_sbuf_tensor("arena", [128, AR], mybir.dt.uint8)
    abase = nc.lookup_mloc(arena).addr

    def at(name, off_kb, shape, dtype):
        return nc.alloc_sbuf_tensor_at(name, shape, dtype,
                                       offset=abase + off_kb * 1024)

    h1_fm = at("h1_fm", 0, [128, KC * NT], BF16)
    xt = [at(f"xt{i}", 32 + 8 * i, [128, D], F32) for i in range(2)]
    sq_sc = at("sq_sc", 48, [128, D], F32)
    h1tm = [at(f"h1tm{i}", 56 + 4 * i, [128, D], BF16) for i in range(2)]
    rp_c = at("rp_c", 32, [128, NT], F32)
    rp_s = at("rp_s", 36, [128, NT], F32)
    kbuf = [at(f"kbuf{i}", 8 * i, [128, 4096], BF16) for i in range(2)]
    vbuf = [at(f"vbuf{i}", 16 + 8 * i, [128, 4096], BF16) for i in range(2)]
    q2_f8 = at("q2_f8", 0, [128, KC * D], FP8)
    x2_sb = at("x2_sb", 32, [128, NTT * D], F32)
    i2st = at("i2st", 96, [128, D], F32)
    q1tm8 = at("q1tm8", 96, [128, D], FP8)
    q1tmb = at("q1tmb", 98, [128, D], BF16)
    ost = at("ost", 102, [128, D], F32)
    wb0 = nc.lookup_mloc(wbuf[0]).addr
    q1w8 = nc.alloc_sbuf_tensor_at("q1w8", [128, 16 * NT], FP8, offset=wb0)
    q1w8v = q1w8.rearrange("p (c u i t) -> p c u i t", c=8, u=NTT, i=2, t=128)

    S = {n: nc.alloc_semaphore(n) for n in [
        "x", "tr", "wg", "pe2", "ve2", "kvw", "cc", "kvl", "as_", "ax", "msk",
        "avs", "smv", "ve3", "bcc", "bc", "nrm", "ph", "pe4", "ve4", "i2l",
        "q2", "pe5", "ve5", "tr5", "ln", "lnw", "st", "xr", "veh",
        "sx0", "sx1", "wg0", "wg1", "kw0", "kw1", "vw0", "vw1", "kl0", "kl1",
        "t50", "t51", "tr0", "tr1"]}

    c_x = Ctr(S["x"], 16)
    c_sx = [Ctr(S["sx0"], 16), Ctr(S["sx1"], 16)]
    c_tr = [Ctr(S["tr0"], 16), Ctr(S["tr1"], 16)]
    c_wg = [Ctr(S["wg0"], 16), Ctr(S["wg1"], 16)]
    c_kw = [Ctr(S["kw0"], 16), Ctr(S["kw1"], 16)]
    c_vw = [Ctr(S["vw0"], 16), Ctr(S["vw1"], 16)]
    c_t5 = [Ctr(S["t50"], 16), Ctr(S["t51"], 16)]
    c_pe2 = Ctr(S["pe2"])
    c_ve2 = Ctr(S["ve2"])
    c_kvw = Ctr(S["kvw"], 16)
    c_cc = Ctr(S["cc"])
    c_ln = Ctr(S["ln"])
    c_lnw = Ctr(S["lnw"])
    c_veh = Ctr(S["veh"])

    def q_sl(h, a, b):
        return q_fm[:, h * NT + a: h * NT + b]

    def av_sl(h, a, b):
        if h == 0:
            return av0[:, a:b]
        return q_fm[:, (h - 1) * NT + a: (h - 1) * NT + b]

    def q1_sl(c, a, b):
        w = wbuf[c // 8]
        cc_ = c % 8
        return w[:, cc_ * NT + a: cc_ * NT + b]

    TE, VE, SE, SY, GP = nc.tensor, nc.vector, nc.scalar, nc.sync, nc.gpsimd

    def HOP(inst):
        c_veh.inc(inst)
        VE.wait_ge(S["veh"], c_veh.v)
        return inst

    for dst, src in [(cos_t, cos_h), (sin_t, sin_h), (masks, mask_h),
                     (onesb, ones_h), (onesf, onesf_h),
                     (scl, scl_h), (scl1, scl1_h)]:
        c_x.inc(SY.dma_start(out=dst[:], in_=src[:]))
    CONST_LOADS = c_x.v

    VE.wait_ge(S["x"], CONST_LOADS)
    VE.memset(nbias[:], EXP_BIAS)
    HOP(VE.tensor_mul(stt["t0"][:], scl[:], scl1[:]))
    c_ve2.inc(VE.reciprocal(rr1[:], stt["t0"][:]))

    for tt in range(NTT):
        xcur = xt[tt % 2]
        if tt >= 2:
            SY.wait_ge(S["lnw"], tt - 1)
        c_sx[tt % 2].inc(SY.dma_start(out=xcur[:], in_=x_h[tt * 128:(tt + 1) * 128, :]))

        SE.wait_ge(S["x"], CONST_LOADS)
        SE.wait_ge(c_sx[tt % 2].sem, c_sx[tt % 2].v)
        if tt > 0:
            SE.wait_ge(S["lnw"], tt)
        c_ln.inc(SE.activation(sq_sc[:], xcur[:], ACT.Square, accum_out=stt["sq"][:]))

        VE.wait_ge(c_sx[tt % 2].sem, c_sx[tt % 2].v)
        VE.wait_ge(S["ln"], c_ln.v)
        HOP(VE.reduce_sum(stt["sum"][:], xcur[:], axis=AX))
        HOP(VE.tensor_scalar(stt["mu"][:], stt["sum"][:], 1.0 / D, None, ALU.mult))
        HOP(VE.tensor_scalar(stt["nmu"][:], stt["mu"][:], -1.0, None, ALU.mult))
        HOP(VE.tensor_mul(stt["t0"][:], stt["mu"][:], stt["mu"][:]))
        HOP(VE.tensor_scalar(stt["t1"][:], stt["sq"][:], 1.0 / D, None, ALU.mult))
        HOP(VE.tensor_sub(stt["ve"][:], stt["t1"][:], stt["t0"][:]))
        HOP(VE.tensor_scalar(stt["ve"][:], stt["ve"][:], 1e-5, None, ALU.add))

        c_ve2.inc(VE.engine_nop())
        SE.wait_ge(S["ve2"], c_ve2.v)
        c_ln.inc(SE.activation(stt["t0"][:], stt["ve"][:], ACT.Sqrt))
        VE.wait_ge(S["ln"], c_ln.v)
        HOP(VE.reciprocal(stt["rs"][:], stt["t0"][:]))
        HOP(VE.tensor_mul(stt["t0"][:], stt["rs"][:], stt["rs"][:]))
        HOP(VE.tensor_mul(stt["t1"][:], stt["t0"][:], stt["ve"][:]))
        HOP(VE.tensor_scalar(stt["t1"][:], stt["t1"][:], -0.5, 1.5, ALU.mult, ALU.add))
        HOP(VE.tensor_mul(stt["rs"][:], stt["rs"][:], stt["t1"][:]))

        c_ve2.inc(VE.engine_nop())
        VE.wait_ge(c_tr[tt % 2].sem, c_tr[tt % 2].v)
        c_lnw.inc(VE.tensor_scalar(h1tm[tt % 2][:], xcur[:], stt["nmu"][:],
                                   stt["rs"][:], ALU.add, ALU.mult))
        SE.wait_ge(S["lnw"], c_lnw.v)
        for c in range(KC):
            c_tr[tt % 2].inc(SE.dma_start_transpose(
                h1_fm[:, c * NT + tt * 128: c * NT + (tt + 1) * 128],
                h1tm[tt % 2][:, c * 128:(c + 1) * 128]))
    H1_TR = (c_tr[0].v, c_tr[1].v)

    OGS = [4, 5, 6, 7, 8, 9, 10, 11, 0, 1, 2, 3]

    def emit_wload(src_ap, og, buf):
        for kg in range(4):
            src = src_ap[kg * 512:(kg + 1) * 512, og * 512:(og + 1) * 512] \
                .rearrange("(k p) c -> p k c", p=128)
            dst = wbuf[buf][:, kg * 4 * 512:(kg + 1) * 4 * 512] \
                .rearrange("p (k c) -> p k c", k=4)
            c_wg[buf].inc(GP.dma_start(out=dst, in_=src))

    emit_wload(wqkv_h, OGS[0], 0)
    emit_wload(wqkv_h, OGS[1], 1)

    with nc.psum_tensor("ps_qk0", [128, NT], F32) as pqk0, \
         nc.psum_tensor("ps_qk1", [128, NT], F32) as pqk1, \
         nc.psum_tensor("ps_v0", [128, 512], F32) as pv0, \
         nc.psum_tensor("ps_v1", [128, 512], F32) as pv1:
        pqk = [pqk0, pqk1]
        pv = [pv0, pv1]
        n_kvw = 0
        n_kvw_at_ag = [48]
        last_qk_evac = [0, 0]
        last_rope_add = 0
        last_v_evac = [0, 0]
        last_kst_dma = [0, 0]
        last_vst_dma = [0, 0]
        qk_i = 0
        v_i = 0

        TE.wait_ge(c_tr[0].sem, H1_TR[0])
        TE.wait_ge(c_tr[1].sem, H1_TR[1])

        for ogi, og in enumerate(OGS):
            buf = ogi % 2
            TE.wait_ge(c_wg[buf].sem, 16 * 4 * (ogi // 2 + 1))
            if 8 <= og <= 11:
                vhb = (og - 8) * 4
                vsec = kv_own[KV_HALF + vhb * PH: KV_HALF + (vhb + 4) * PH] \
                    .rearrange("(h t d) -> h t d", h=4, d=128)
                for tt in range(NTT):
                    par = v_i % 2
                    if last_v_evac[par]:
                        TE.wait_ge(S["ve2"], last_v_evac[par])
                    for k in range(KC):
                        mm = TE.matmul(
                            pv[par][:],
                            h1_fm[:, k * NT + tt * 128: k * NT + (tt + 1) * 128],
                            wbuf[buf][:, k * 512:(k + 1) * 512],
                            start=(k == 0), stop=(k == KC - 1))
                    c_pe2.inc(mm)
                    VE.wait_ge(S["pe2"], c_pe2.v)
                    sp = tt % 2
                    if last_vst_dma[sp]:
                        VE.wait_ge(c_vw[sp].sem, last_vst_dma[sp])
                    last_v_evac[par] = c_ve2.inc(VE.tensor_copy(vst[sp][:], pv[par][:]))
                    SY.wait_ge(S["ve2"], c_ve2.v)
                    c_vw[sp].inc(SY.dma_start(
                        out=vsec[:, tt * 128:(tt + 1) * 128, :].transpose([1, 0, 2]),
                        in_=vst[sp][:].rearrange("p (h d) -> p h d", h=4)))
                    last_vst_dma[sp] = c_vw[sp].v
                    n_kvw += 1
                    v_i += 1
            else:
                for jj in range(4):
                    par = qk_i % 2
                    if last_qk_evac[par]:
                        TE.wait_ge(S["ve2"], last_qk_evac[par])
                    for tb in range(2):
                        for k in range(KC):
                            mm = TE.matmul(
                                pqk[par][:, tb * 512:(tb + 1) * 512],
                                wbuf[buf][:, k * 512 + jj * 128: k * 512 + jj * 128 + 128],
                                h1_fm[:, k * NT + tb * 512: k * NT + (tb + 1) * 512],
                                start=(k == 0), stop=(k == KC - 1))
                        c_pe2.inc(mm)
                    VE.wait_ge(S["pe2"], c_pe2.v)
                    src = pqk[par]
                    if last_rope_add:
                        VE.wait_ge(S["ve2"], last_rope_add)
                    VE.tensor_mul(rp_c[:], src[:], cos_t[:])
                    VE.tensor_mul(rp_s[0:64, :], src[64:128, :], sin_t[0:64, :])
                    HOP(VE.tensor_mul(rp_s[64:128, :], src[0:64, :], sin_t[64:128, :]))
                    if og >= 4:
                        h = (og - 4) * 4 + jj
                        sp = h % 2
                        if last_kst_dma[sp]:
                            VE.wait_ge(c_kw[sp].sem, last_kst_dma[sp])
                        last_qk_evac[par] = c_ve2.inc(
                            VE.tensor_add(kst[sp][:], rp_c[:], rp_s[:]))
                        last_rope_add = c_ve2.v
                        SY.wait_ge(S["ve2"], c_ve2.v)
                        c_kw[sp].inc(SY.dma_start(
                            out=kv_own[h * PH: (h + 1) * PH]
                            .rearrange("(d t) -> d t", d=128),
                            in_=kst[sp][:]))
                        last_kst_dma[sp] = c_kw[sp].v
                        n_kvw += 1
                    else:
                        h = og * 4 + jj
                        last_qk_evac[par] = c_ve2.inc(
                            VE.tensor_add(q_sl(h, 0, NT), rp_c[:], rp_s[:]))
                        last_rope_add = c_ve2.v
                    qk_i += 1
            if ogi + 2 < len(OGS):
                GP.wait_ge(S["pe2"], c_pe2.v)
                emit_wload(wqkv_h, OGS[ogi + 2], buf)
            if ogi == 9:
                GP.wait_ge(c_kw[0].sem, c_kw[0].v)
                GP.wait_ge(c_kw[1].sem, c_kw[1].v)
                GP.wait_ge(c_vw[0].sem, c_vw[0].v)
                GP.wait_ge(c_vw[1].sem, c_vw[1].v)
                c_cc.inc(GP.collective_compute(
                    "AllGather", ALU.bypass, replica_groups=RGROUPS,
                    ins=[kv_own[:]], outs=[kv_all[:]]))

    QKV_VE = c_ve2.v
    QKV_PE = c_pe2.v

    engines = [TE, VE, SE, SY, GP]
    pid = nc.partition_id()
    idx = []
    for eng in engines:
        reg = eng.alloc_register(f"jreg_{eng.engine.value}")
        eng.reg_mov(reg, pid)
        eng.reg_alu(reg, reg, 4, ALU.mod)
        idx.append(RuntimeValue(reg, min_val=0, max_val=3))

    def emit_wo(db, buf):
        for kg in range(4):
            src = wo_h[kg * 512:(kg + 1) * 512, db * 512:(db + 1) * 512] \
                .rearrange("(k p) c -> p k c", p=128)
            dst = wbuf[buf][:, kg * 4 * 512:(kg + 1) * 4 * 512] \
                .rearrange("p (k c) -> p k c", k=4)
            c_wg[buf].inc(GP.dma_start(out=dst, in_=src))

    WO_B = (c_wg[0].v, c_wg[1].v)

    with nc.psum_tensor("ps_s0", [128, 512], F32) as ps_s0, \
         nc.psum_tensor("ps_s1", [128, 512], F32) as ps_s1, \
         nc.psum_tensor("ps_a0", [128, 512], F32) as ps_a0, \
         nc.psum_tensor("ps_a1", [128, 512], F32) as ps_a1, \
         nc.psum_tensor("ps_m0", [1, 512], F32) as ps_m0, \
         nc.psum_tensor("ps_m1", [1, 512], F32) as ps_m1, \
         nc.psum_tensor("ps_bc", [128, 512], F32) as ps_bc:
        ps_s = [ps_s0, ps_s1]
        ps_a = [ps_a0, ps_a1]
        ps_m = [ps_m0, ps_m1]

        VEH_BASE = c_veh.v
        veh_body_counts = []
        for j in nc.Switch(engines=engines, index=idx, n=4):
            a_veh = Ctr(S["veh"])
            a_veh.v = VEH_BASE

            def HOPB(inst):
                a_veh.inc(inst)
                VE.wait_ge(S["veh"], a_veh.v)
                return inst

            a_kvl = [Ctr(S["kl0"], 16), Ctr(S["kl1"], 16)]
            a_as = Ctr(S["as_"])
            a_ax = Ctr(S["ax"])
            a_msk = Ctr(S["msk"])
            a_avs = Ctr(S["avs"])
            a_smv = Ctr(S["smv"])
            a_ve3 = Ctr(S["ve3"])
            a_bcc = Ctr(S["bcc"])
            a_bc = Ctr(S["bc"])
            a_nrm = Ctr(S["nrm"])
            a_xr = Ctr(S["xr"], 16)

            NBLK = 8 - j
            blocks = [j, 7 - j]
            lph = 2 * NBLK
            av_of_pair = {}
            pend_bcast = []

            def emit_kv_loads(h):
                hb = h % 2
                for g in range(NBLK):
                    r = g if g < 4 else 7 - g
                    half = 0 if g < 4 else 1
                    ksec = kv_all[r, h * PH: (h + 1) * PH] \
                        .rearrange("(d t) -> d t", d=128)
                    a_kvl[hb].inc(SY.dma_start(
                        out=kbuf[hb][:, g * 512:(g + 1) * 512],
                        in_=ksec[:, half * 512:(half + 1) * 512]))
                    vsec = kv_all[r, KV_HALF + h * PH: KV_HALF + (h + 1) * PH] \
                        .rearrange("(t d) -> t d", d=128)
                    a_kvl[hb].inc(SY.dma_start(
                        out=vbuf[hb][:, g * 512:(g + 1) * 512]
                        .rearrange("p (q d) -> p q d", q=4),
                        in_=vsec[half * 512:(half + 1) * 512, :]
                        .rearrange("(q p) d -> p q d", p=128)))

            def emit_bcast(p):
                h_, qb_, par_ = av_of_pair[p]
                TE.wait_ge(S["ve3"], p + 1)
                if p > 0:
                    TE.wait_ge(S["bcc"], p)
                a_bc.inc(TE.matmul(ps_bc[:], onesf[:, :], recip_sb[:, :],
                                   start=True, stop=True, skip_group_check=True))
                VE.wait_ge(S["bc"], a_bc.v)
                if a_nrm.v:
                    VE.wait_ge(S["nrm"], a_nrm.v)
                HOPB(VE.tensor_copy(bc_sb[:], ps_bc[:]))
                a_bcc.inc(VE.engine_nop())
                qoff = qb_ * 512
                a_nrm.inc(VE.tensor_mul(av_sl(h_, qoff, qoff + 512),
                                        ps_a[par_][:], bc_sb[:]))

            SY.wait_ge(S["cc"], 1)
            SY.wait_ge(S["pe2"], QKV_PE)
            SY.wait_ge(S["ve2"], QKV_VE)
            SE.wait_ge(S["ve2"], QKV_VE)
            emit_kv_loads(0)
            emit_kv_loads(1)

            gi = 0
            pairidx = 0
            xr_tt = 0
            pend_av = [None]

            def flush_av():
                if pend_av[0] is not None:
                    pend_av[0]()
                    pend_av[0] = None

            for h in range(H):
                TE.wait_ge(a_kvl[h % 2].sem, 16 * lph * (h // 2 + 1))
                for qbi, jb in enumerate(blocks):
                    P = 4 * (jb + 1)
                    par = pairidx % 2
                    qoff = qbi * 512
                    while pend_bcast and pend_bcast[0] <= pairidx - 2:
                        emit_bcast(pend_bcast.pop(0))
                    first_gi = gi
                    for kt in range(P):
                        if gi >= 2:
                            TE.wait_ge(S["ax"], gi - 1)
                        a_as.inc(TE.matmul(ps_s[gi % 2][:],
                                           kbuf[h % 2][:, kt * 128:(kt + 1) * 128],
                                           q_sl(h, qoff, qoff + 512),
                                           start=True, stop=True,
                                           skip_group_check=True))
                        SE.wait_ge(S["as_"], a_as.v)
                        if gi >= 3:
                            SE.wait_ge(S["avs"], 2 * (gi - 2))
                        a_ax.inc(SE.activation(expT[gi % 3][:], ps_s[gi % 2][:],
                                               ACT.Exp, bias=nbias[:], scale=SCALE_S))
                        diag = kt >= P - 4
                        if diag:
                            dd = kt - (P - 4)
                            VE.wait_ge(S["ax"], a_ax.v)
                            a_msk.inc(VE.tensor_mul(
                                expT[gi % 3][:], expT[gi % 3][:],
                                masks[:, dd * 512:(dd + 1) * 512]))
                        flush_av()
                        a_avs.v += 2

                        def em(gi_=gi, par_=par, kt_=kt, P_=P, h_=h,
                               pairidx_=pairidx, diag_=diag,
                               ax_v=a_ax.v, msk_v=a_msk.v):
                            if diag_:
                                TE.wait_ge(S["msk"], msk_v)
                            else:
                                TE.wait_ge(S["ax"], ax_v)
                            if kt_ == 0 and pairidx_ >= 2:
                                TE.wait_ge(S["nrm"], pairidx_ - 1)
                            TE.matmul(ps_a[par_][:],
                                      vbuf[h_ % 2][:, kt_ * 128:(kt_ + 1) * 128],
                                      expT[gi_ % 3][:],
                                      start=(kt_ == 0), stop=(kt_ == P_ - 1),
                                      skip_group_check=True) \
                                .then_inc(S["avs"], 1)
                            if kt_ == 0 and pairidx_ >= 2:
                                TE.wait_ge(S["smv"], pairidx_ - 1)
                            TE.matmul(ps_m[par_][:], onesb[:, 0:1],
                                      expT[gi_ % 3][:],
                                      start=(kt_ == 0), stop=(kt_ == P_ - 1),
                                      skip_group_check=True) \
                                .then_inc(S["avs"], 1)
                        pend_av[0] = em
                        gi += 1
                        if pend_bcast and (gi - first_gi) == 8:
                            emit_bcast(pend_bcast.pop(0))
                    while pend_bcast and pend_bcast[0] <= pairidx - 1:
                        emit_bcast(pend_bcast.pop(0))
                    SE.wait_ge(S["avs"], a_avs.v)
                    if a_ve3.v:
                        SE.wait_ge(S["ve3"], a_ve3.v)
                        SE.wait_ge(S["bc"], a_bc.v)
                    a_smv.inc(SE.activation(sums_sb[:], ps_m[par][:], ACT.Ln))
                    SE.wait_ge(S["smv"], a_smv.v)
                    a_ve3.inc(SE.activation(recip_sb[:], sums_sb[:], ACT.Exp,
                                            scale=-1.0))
                    av_of_pair[pairidx] = (h, qbi, par)
                    pend_bcast.append(pairidx)
                    pairidx += 1
                if h + 2 < H:
                    SY.wait_ge(S["avs"], a_avs.v)
                    emit_kv_loads(h + 2)
                if xr_tt < NTT:
                    a_xr.inc(SY.dma_start(
                        out=x2_sb[:, xr_tt * D:(xr_tt + 1) * D],
                        in_=x_h[xr_tt * 128:(xr_tt + 1) * 128, :]))
                    xr_tt += 1
            flush_av()
            while pend_bcast:
                emit_bcast(pend_bcast.pop(0))

            TE.wait_ge(S["bc"], a_bc.v)
            TE.sem_inc(S["ph"], 1)
            VE.wait_ge(S["nrm"], a_nrm.v)
            VE.sem_inc(S["ph"], 1)
            SE.wait_ge(S["ax"], a_ax.v)
            SE.sem_inc(S["ph"], 1)
            SY.wait_ge(a_kvl[0].sem, a_kvl[0].v)
            SY.wait_ge(a_kvl[1].sem, a_kvl[1].v)
            SY.wait_ge(S["xr"], a_xr.v)
            SY.sem_inc(S["ph"], 1)
            GP.wait_ge(S["cc"], 1)
            GP.wait_ge(S["pe2"], QKV_PE)
            c_wg[0].v, c_wg[1].v = WO_B
            emit_wo(0, 0)
            emit_wo(1, 1)
            GP.sem_inc(S["ph"], 1)
            veh_body_counts.append(a_veh.v - VEH_BASE)

    assert len(set(veh_body_counts)) == 1, veh_body_counts
    c_veh.v = VEH_BASE + veh_body_counts[0]
    c_wg[0].v = WO_B[0] + 64
    c_wg[1].v = WO_B[1] + 64

    c_pe4 = Ctr(S["pe4"])
    c_ve4 = Ctr(S["ve4"])
    c_i2l = Ctr(S["i2l"], 16)
    c_q2 = Ctr(S["q2"])

    for eng in engines:
        eng.wait_ge(S["ph"], 5)

    WG0 = [WO_B[0], WO_B[1]]

    q2q = q2_f8.rearrange("p (c nb i n) -> p c i nb n", c=8, nb=4, i=2, n=512)

    def emit_i2(k):
        SE.wait_ge(S["q2"], k)
        c_i2l.inc(SE.dma_start(out=i2st[:], in_=inp2_h[k * 128:(k + 1) * 128, :]))
        VE.wait_ge(S["i2l"], c_i2l.v)
        c_q2.inc(VE.tensor_scalar(q2q[:, k // 2, k % 2, :, :],
                                  i2st.rearrange("p (nb n) -> p nb n", nb=4),
                                  scl1[:], None, ALU.mult))

    next_k = [0]

    def maybe_i2():
        if next_k[0] < KC:
            emit_i2(next_k[0])
            next_k[0] += 1

    with nc.psum_tensor("ps_o0", [128, 512], F32) as ps_o0, \
         nc.psum_tensor("ps_o1", [128, 512], F32) as ps_o1:
        ps_o = [ps_o0, ps_o1]
        ot = 0
        for db in range(4):
            TE.wait_ge(c_wg[db % 2].sem, WG0[db % 2] + 16 * 4 * (db // 2 + 1))
            for tt in range(NTT):
                par = ot % 2
                if ot >= 2:
                    TE.wait_ge(S["ve4"], ot - 1)
                for c in range(KC):
                    mm = TE.matmul(ps_o[par][:],
                                   av_sl(c, tt * 128, (tt + 1) * 128),
                                   wbuf[db % 2][:, c * 512:(c + 1) * 512],
                                   start=(c == 0), stop=(c == KC - 1))
                c_pe4.inc(mm)
                VE.wait_ge(S["pe4"], c_pe4.v)
                VE.wait_ge(S["xr"], 16 * NTT)
                xsl = x2_sb[:, tt * D + db * 512: tt * D + (db + 1) * 512]
                c_ve4.inc(VE.tensor_add(xsl, ps_o[par][:], xsl))
                ot += 1
                if ot % 2 == 0:
                    maybe_i2()
            if db + 2 < 4:
                GP.wait_ge(S["pe4"], c_pe4.v)
                emit_wo(db + 2, db % 2)
    while next_k[0] < KC:
        maybe_i2()

    c_pe5 = Ctr(S["pe5"])
    c_ve5 = Ctr(S["ve5"])
    c_tr5 = Ctr(S["tr5"], 16)
    c_st = Ctr(S["st"], 16)

    with nc.psum_tensor("ps_f0", [128, 512], F32) as ps_f0, \
         nc.psum_tensor("ps_f1", [128, 512], F32) as ps_f1, \
         nc.psum_tensor("ps_sq", [128, D], F32) as ps_sq:
        ps_f = [ps_f0, ps_f1]

        for tt in range(NTT):
            x2t = x2_sb[:, tt * D:(tt + 1) * D]
            SE.wait_ge(S["ve4"], 25 + tt)
            c_ln.inc(SE.activation(ps_sq[:], x2t, ACT.Square,
                                   accum_out=s8_sq[:, tt:tt + 1]))
            SE.wait_ge(S["ln"], c_ln.v)
            VE.wait_ge(S["ve4"], 25 + tt)
            HOP(VE.reduce_sum(s8_sum[:, tt:tt + 1], x2t, axis=AX))
        VE.wait_ge(S["ln"], c_ln.v)
        HOP(VE.tensor_scalar(s8_a[:], s8_sum[:], 1.0 / D, None, ALU.mult))
        HOP(VE.tensor_scalar(s8_nmu[:], s8_a[:], -1.0, None, ALU.mult))
        HOP(VE.tensor_mul(s8_b[:], s8_a[:], s8_a[:]))
        HOP(VE.tensor_scalar(s8_a[:], s8_sq[:], 1.0 / D, None, ALU.mult))
        HOP(VE.tensor_sub(s8_a[:], s8_a[:], s8_b[:]))
        HOP(VE.tensor_scalar(s8_a[:], s8_a[:], 1e-5, None, ALU.add))
        c_ve2.inc(VE.engine_nop())
        SE.wait_ge(S["ve2"], c_ve2.v)
        c_ln.inc(SE.activation(s8_b[:], s8_a[:], ACT.Ln))
        SE.wait_ge(S["ln"], c_ln.v)
        c_ln.inc(SE.activation(s8_a[:], s8_b[:], ACT.Exp, scale=-0.5))
        VE.wait_ge(S["ln"], c_ln.v)
        HOP(VE.tensor_scalar(s8_rss[:], s8_a[:], scl[:], None, ALU.mult))

        q2v = q2_f8.rearrange("p (c nb i n) -> p c nb i n", c=8, nb=4, i=2)
        ft = 0
        for tt in range(NTT):
            x2t = x2_sb[:, tt * D:(tt + 1) * D]
            VE.wait_ge(c_t5[(tt + 1) % 2].sem, c_t5[(tt + 1) % 2].v)
            c_lnw.inc(VE.tensor_scalar(q1tm8[:], x2t, s8_nmu[:, tt:tt + 1],
                                       s8_rss[:, tt:tt + 1], ALU.add, ALU.mult))
            VE.wait_ge(S["lnw"], c_lnw.v)
            c_lnw.inc(VE.tensor_copy(q1tmb[:], q1tm8[:]))
            SE.wait_ge(S["lnw"], c_lnw.v)
            SY.wait_ge(S["lnw"], c_lnw.v)
            if tt == 0:
                SE.wait_ge(S["pe4"], c_pe4.v)
                SY.wait_ge(S["pe4"], c_pe4.v)
            stg = wbuf[1][:, (tt % 2) * D:(tt % 2) * D + D]
            for c in range(KC):
                q = SE if c % 2 == 0 else SY
                c_t5[tt % 2].inc(q.dma_start_transpose(
                    stg[:, c * 128:(c + 1) * 128],
                    q1tmb[:, c * 128:(c + 1) * 128]))
            VE.wait_ge(c_t5[tt % 2].sem, c_t5[tt % 2].v)
            if tt == 0:
                VE.wait_ge(S["pe4"], c_pe4.v)
            stg3 = stg.rearrange("p (c i t) -> p c i t", c=8, i=2)
            for i in range(2):
                c_lnw.inc(VE.tensor_copy(q1w8v[:, :, tt, i, :],
                                         stg3[:, :, i, :]))
            TE.wait_ge(S["lnw"], c_lnw.v)
            if tt == 0:
                TE.wait_ge(S["q2"], KC)
            for nb in range(4):
                par = ft % 2
                if ft >= 2:
                    TE.wait_ge(S["ve5"], ft - 1)
                for cp in range(8):
                    mm = TE.matmul(ps_f[par][:],
                                   q1w8v[:, cp, tt, :, :],
                                   q2v[:, cp, nb, :, :],
                                   start=(cp == 0), stop=(cp == 7),
                                   perf_mode=mybir.MatmulPerfMode.DoubleRow)
                c_pe5.inc(mm)
                VE.wait_ge(S["pe5"], c_pe5.v)
                if tt > 0:
                    VE.wait_ge(S["st"], 16 * tt)
                c_ve5.inc(VE.scalar_tensor_tensor(
                    ost[:, nb * 512:(nb + 1) * 512], ps_f[par][:], rr1[:],
                    x2t[:, nb * 512:(nb + 1) * 512], ALU.mult, ALU.add))
                ft += 1
            SY.wait_ge(S["ve5"], c_ve5.v)
            c_st.inc(SY.dma_start(out=y_h[tt * 128:(tt + 1) * 128, :], in_=ost[:]))
        SY.wait_ge(S["st"], c_st.v)

    ctx.pop_all()
    return nc


_NC_CACHE = []


def _rope_tables():
    freqs = np.exp(np.float32(-math.log(10000.0)) *
                   (np.arange(0, 128, 2, dtype=np.float32) /
                    np.float32(128.0))).astype(np.float32)
    pos = np.arange(T, dtype=np.float32)
    ang = (freqs[:, None] * pos[None, :]).astype(np.float32)
    sin = np.sin(ang).astype(np.float32)
    cos = np.cos(ang).astype(np.float32)
    cos_full = np.concatenate([cos, cos], axis=0)
    sin_signed = np.concatenate([-sin, sin], axis=0)
    return cos_full, sin_signed


def _perm_cols():
    r = np.arange(128)
    within = np.where(r < 64, 2 * r, 2 * (r - 64) + 1)
    return np.concatenate([h * 128 + within for h in range(H)])


def kernel(L0_attn_x, L0_attn_Wqkv, L0_attn_WO, L0_ffn_input2,
           L0_ffn_scale, L0_ffn_scale1):
    x = np.ascontiguousarray(np.asarray(L0_attn_x, dtype=np.float32))
    wqkv = np.asarray(L0_attn_Wqkv, dtype=np.float32)
    wo = np.asarray(L0_attn_WO, dtype=np.float32)
    inp2 = np.ascontiguousarray(np.asarray(L0_ffn_input2, dtype=np.float32))
    s = np.float32(np.asarray(L0_ffn_scale))
    s1 = np.float32(np.asarray(L0_ffn_scale1))

    wqkvT = np.ascontiguousarray(wqkv.T)
    perm = _perm_cols()
    wqkvT_p = wqkvT.copy()
    wqkvT_p[:, :D] = wqkvT[:, :D][:, perm]
    wqkvT_p[:, D:2 * D] = wqkvT[:, D:2 * D][:, perm]
    wqkvT_p = np.ascontiguousarray(wqkvT_p)
    woT = np.ascontiguousarray(wo.T)

    cos_full, sin_signed = _rope_tables()

    kk = np.arange(128)[:, None]
    qq = np.arange(512)[None, :]
    masks = np.concatenate(
        [(qq >= (d * 128 + kk)).astype(ml_dtypes.bfloat16) for d in range(4)],
        axis=1)
    onesb = np.ones((128, 128), dtype=ml_dtypes.bfloat16)
    onesf = np.ones((1, 128), dtype=np.float32)
    scl_b = np.full((128, 1), s, dtype=np.float32)
    scl1_b = np.full((128, 1), s1, dtype=np.float32)

    if not _NC_CACHE:
        _NC_CACHE.append(build_nc())
    nc = _NC_CACHE[0]

    in_maps = []
    for c in range(8):
        b, j = c // 4, c % 4
        xo = np.concatenate([x[b, j * 512:(j + 1) * 512],
                             x[b, (7 - j) * 512:(8 - j) * 512]], axis=0)
        gpos = np.concatenate([np.arange(j * 512, (j + 1) * 512),
                               np.arange((7 - j) * 512, (8 - j) * 512)])
        in_maps.append({
            "x_own": np.ascontiguousarray(xo),
            "wqkvT": wqkvT_p,
            "woT": woT,
            "inp2": inp2,
            "cosT": np.ascontiguousarray(cos_full[:, gpos]),
            "sinT": np.ascontiguousarray(sin_signed[:, gpos]),
            "maskT": masks,
            "onesb": onesb,
            "onesf": onesf,
            "scale_b": scl_b,
            "scale1_b": scl1_b,
        })

    res = run_bass_kernel_spmd(nc, in_maps, core_ids=list(range(8)))

    y = np.empty((B, T, D), dtype=np.float32)
    for c in range(8):
        b, j = c // 4, c % 4
        yo = res.results[c]["y_own"]
        y[b, j * 512:(j + 1) * 512] = yo[:512]
        y[b, (7 - j) * 512:(8 - j) * 512] = yo[512:]
    return y
